# version 01
# baseline (speedup 1.0000x reference)
import contextlib
import numpy as np
import concourse.bass as bass
import concourse.mybir as mybir
from concourse.bass_utils import run_bass_kernel_spmd

F32 = mybir.dt.float32
BF16 = mybir.dt.bfloat16
AF = mybir.ActivationFunctionType
ALU = mybir.AluOpType
AXX = mybir.AxisListType.X

P = 128
T = 1024
NT = 8
D = 2048
KT = 16
II = 1408
IT = 11
NE = 9
DIN = 96
DST = 16
DTR = 6
EPS = 1e-5
NCORES = 8

ENGS = ("tensor", "vector", "scalar", "sync", "gpsimd")


class Sem:
    def __init__(self, es, nc, name):
        self.h = es.enter_context(nc.semaphore(name))
        self.n = 0

    def bump(self, k=1):
        self.n += k
        return self.n


class Plan:
    def __init__(self):
        self.q = {k: [] for k in ENGS}

    def wait(self, eng, sem, val):
        if val <= 0:
            return
        self.q[eng].append(lambda e, s=sem.h, v=val: e.wait_ge(s, v))

    def op(self, eng, build, inc=None, inc2=None):
        if inc is None:
            self.q[eng].append(build)
            return None
        sem, k = inc
        v = sem.bump(k)
        self.q[eng].append(lambda e, b=build, s=sem.h, kk=k: b(e).then_inc(s, kk))
        if inc2 is not None:
            sem2, k2 = inc2
            sem2.bump(k2)
            self.q[eng].append(
                lambda e, s2=sem2.h, kk2=k2: e.nop().then_inc(s2, kk2))
        return v

    def raw(self, eng, fn):
        self.q[eng].append(fn)


def build_program(do_mamba=True, do_ffn2=True, n_experts=NE):
    nc = bass.Bass("TRN2", target_bir_lowering=False)

    def din(name, shape):
        return nc.dram_tensor(name, list(shape), F32, kind="ExternalInput").ap()

    dx = din("x", [T, D])
    dwg = din("wgT", [NE, D, II])
    dwu = din("wuT", [NE, D, II])
    dwd = din("wdT", [NE, II, D])
    dgate = din("gateT", [D, 8])
    dinp = din("inpT", [D, DIN])
    dxpB = din("xpB", [DIN, DST])
    dxpC = din("xpC", [DIN, DST])
    dxpd = din("xpd", [DIN, DTR])
    ddtT = din("dtT", [DTR, DIN])
    ddtb = din("dtb", [DIN, 1])
    dconvw = din("convw", [DIN, 3])
    dconvb = din("convb", [DIN, 1])
    dAlog = din("Alog", [DIN, DST])
    dDp = din("Dp", [DIN, 1])
    doutT = din("outT", [DIN, D])
    dlnw = din("lnw_bc", [P, D])
    dlnb = din("lnb_bc", [P, D])
    dident = din("ident", [P, P])
    dselc = din("selc", [DIN, 3])
    dselL = din("selL", [DIN, 4])
    dselR = din("selR", [DIN, 4])
    dy = nc.dram_tensor("y", [T, D], F32, kind="ExternalOutput").ap()
    dag1i = nc.dram_tensor("ag1i", [DIN, 2], F32, kind="Internal").ap()
    dag1o = nc.dram_tensor("ag1o", [4 * DIN, 2], F32, kind="Internal").ap()
    dag2i = nc.dram_tensor("ag2i", [DIN, 2 * DST], F32, kind="Internal").ap()
    dag2o = nc.dram_tensor("ag2o", [4 * DIN, 2 * DST], F32, kind="Internal").ap()
    dbpcp = nc.dram_tensor("bpcp", [2 * DST, T], F32, kind="Internal").ap()
    groups = [[0, 1, 2, 3], [4, 5, 6, 7]]

    pl = Plan()
    es = contextlib.ExitStack()
    with es:
        sb = lambda name, shape, dt: es.enter_context(
            nc.sbuf_tensor(name, list(shape), dt))
        xr = sb("xr", [P, NT * D], F32)
        lnT = sb("lnT", [P, KT * T], BF16)
        wg2 = sb("wg2", [P, 2 * KT * P], BF16)
        wu2 = sb("wu2", [P, 2 * KT * P], BF16)
        ident_s = sb("ident_s", [P, P], BF16)
        gate_s = sb("gate_s", [P, KT * 8], BF16)
        sg2 = sb("sg2", [P, 2 * 512], F32)
        lg_s = sb("lg_s", [P, NT * 8], F32)
        m1_s = sb("m1_s", [P, NT], F32)
        m2_s = sb("m2_s", [P, NT], F32)
        e2_s = sb("e2_s", [P, NT], F32)
        w1_s = sb("w1_s", [P, NT], F32)
        w2_s = sb("w2_s", [P, NT], F32)
        eq1_s = sb("eq1_s", [P, NT * 8], F32)
        msk_s = sb("msk_s", [P, NT * 8], F32)
        comb = sb("comb", [P, NT * 8], F32)
        fsum = sb("fsum", [P, NT], F32)
        fsq = sb("fsq", [P, NT], F32)
        fmu = sb("fmu", [P, NT], F32)
        fvar = sb("fvar", [P, NT], F32)
        ftmp = sb("ftmp", [P, NT], F32)
        frstd = sb("frstd", [P, NT], F32)
        fnmu = sb("fnmu", [P, NT], F32)
        mmu = sb("mmu", [P, NT], F32)
        mvar = sb("mvar", [P, NT], F32)
        mrstd = sb("mrstd", [P, NT], F32)
        mtmp = sb("mtmp", [P, NT], F32)
        mseff = sb("mseff", [P, NT], F32)
        mnb = sb("mnb", [P, NT], F32)
        mnmr = sb("mnmr", [P, NT], F32)
        xpB_s = sb("xpB_s", [DIN, DST], F32)
        xpC_s = sb("xpC_s", [DIN, DST], F32)
        xpd_s = sb("xpd_s", [DIN, DTR], F32)
        dtT_s = sb("dtT_s", [DTR, DIN], F32)
        dtb_s = sb("dtb_s", [DIN, 1], F32)
        convw_s = sb("convw_s", [DIN, 3], F32)
        convb_s = sb("convb_s", [DIN, 1], F32)
        Alog_s = sb("Alog_s", [DIN, DST], F32)
        negA = sb("negA", [DIN, DST], F32)
        Dp_s = sb("Dp_s", [DIN, 1], F32)
        selc_s = sb("selc_s", [DIN, 3], F32)
        selL_s = sb("selL_s", [DIN, 4], F32)
        selR_s = sb("selR_s", [DIN, 4], F32)
        inp_s = sb("inp_s", [P, KT * DIN], BF16)
        outT_s = sb("outT_s", [DIN, D], BF16)
        F_s = sb("F_s", [DIN, DST], F32)
        Atot_s = sb("Atot_s", [DIN, DST], F32)
        carry = sb("carry", [DIN, DST], F32)
        cs1 = sb("cs1", [DIN, DST], F32)
        cs2 = sb("cs2", [DIN, DST], F32)
        cs3 = sb("cs3", [DIN, DST], F32)
        ctmp = sb("ctmp", [DIN, DST], F32)
        slots2 = sb("slots2", [DIN, 4 * 2 * DST], F32)
        slots1 = sb("slots1", [DIN, 4 * 2], F32)
        Stot_s = sb("Stot_s", [DIN, 1], F32)

        pA = es.enter_context(nc.psum_tensor("pA", [P, 2048], F32))
        pB = es.enter_context(nc.psum_tensor("pB", [P, 1024], F32))
        pC = es.enter_context(nc.psum_tensor("pC", [P, 512], F32))
        pD = es.enter_context(nc.psum_tensor("pD", [P, 512], F32))
        pB16 = pB[:].bitcast(BF16)
        pC16 = pC[:].bitcast(BF16)
        pD16 = pD[:].bitcast(BF16)
        tslots = [pB16[:, 0:P], pB16[:, 1024:1024 + P],
                  pC16[:, 0:P], pD16[:, 2 * P:3 * P]]

        S = {}
        for name in ("ld", "gld", "wg", "wgf", "wu", "wuf", "wd", "wdf",
                     "pg", "pu", "sg", "h", "pd", "xw", "trp", "tre",
                     "lntok", "rt", "rcp", "stat",
                     "mA", "mB", "mC", "mD", "mE", "mF", "mG", "mH",
                     "bcl", "bcf", "ag", "st"):
            S[name] = Sem(es, nc, "s_" + name)

        sync_loads = [
            (xr[:].rearrange("p (a f) -> p a f", a=NT),
             dx.rearrange("(a p) f -> p a f", p=P)),
            (xpB_s[:], dxpB), (xpC_s[:], dxpC), (xpd_s[:], dxpd),
            (dtT_s[:], ddtT), (dtb_s[:], ddtb), (convw_s[:], dconvw),
            (convb_s[:], dconvb), (Alog_s[:], dAlog), (Dp_s[:], dDp),
            (selc_s[:], dselc), (selL_s[:], dselL), (selR_s[:], dselR),
        ]
        for dst, src in sync_loads:
            pl.op("sync", lambda e, d_=dst, s_=src: e.dma_start(out=d_, in_=s_),
                  inc=(S["ld"], 16))
        LD0 = S["ld"].n
        g_loads = [
            (ident_s[:], dident),
            (gate_s[:].rearrange("p (a f) -> p a f", a=KT),
             dgate.rearrange("(a p) f -> p a f", p=P)),
            (inp_s[:].rearrange("p (a f) -> p a f", a=KT),
             dinp.rearrange("(a p) f -> p a f", p=P)),
            (outT_s[:], doutT),
        ]
        for dst, src in g_loads:
            pl.op("gpsimd", lambda e, d_=dst, s_=src: e.dma_start(out=d_, in_=s_),
                  inc=(S["gld"], 16))
        GLD0 = S["gld"].n

        pl.wait("scalar", S["ld"], LD0)
        pl.op("scalar", lambda e: e.activation(negA[:], Alog_s[:], AF.Exp),
              inc=(S["stat"], 1))
        pl.wait("vector", S["stat"], S["stat"].n)
        pl.op("vector", lambda e: e.tensor_scalar_mul(negA[:], negA[:], -1.0),
              inc=(S["stat"], 1))

        C = dict(ug=0, ud=0, gsl=0, usl=0, wsl=0, ne=0, tr=0)
        state = dict(stt_val=[], tile_last_stt={})

        def emit_transposes(src_tile_fn, nt_tiles, prep_fn, xw_entry, rcp_entry):
            base = S["tre"].n
            first = True
            for tt in range(nt_tiles):
                psem, pval = prep_fn(tt)
                pl.wait("tensor", psem, pval)
                if first:
                    pl.wait("tensor", S["xw"], xw_entry)
                    pl.wait("tensor", S["rcp"], rcp_entry)
                    first = False
                for kt in range(KT):
                    k = C["tr"]
                    sl = k % 4
                    if k >= 4:
                        pl.wait("tensor", S["tre"], k - 3)
                    pl.op("tensor", lambda e, s_=src_tile_fn(tt, kt), d_=tslots[sl]:
                          e.transpose(d_, s_, ident_s[:]), inc=(S["trp"], 1))
                    eng = "scalar" if (kt % 2 == 0) else "vector"
                    pl.wait(eng, S["trp"], S["trp"].n)
                    dst = lnT[:, kt * T + tt * P: kt * T + (tt + 1) * P]
                    if eng == "scalar":
                        pl.op(eng, lambda e, d_=dst, s_=tslots[sl]: e.copy(d_, s_),
                              inc=(S["tre"], 1))
                    else:
                        pl.op(eng, lambda e, d_=dst, s_=tslots[sl]:
                              e.tensor_copy(d_, s_), inc=(S["tre"], 1))
                    C["tr"] += 1
            return base

        def ffn_call(ph, xw_entry, rcp_entry):
            xw_base = S["xw"].n

            pl.wait("vector", S["ld"], 16)
            for tt in range(NT):
                pl.op("vector", lambda e, t=tt: e.reduce_sum(
                    fsum[:, t:t + 1], xr[:, t * D:(t + 1) * D], axis=AXX),
                    inc=(S["stat"], 1))
            pl.wait("scalar", S["ld"], 16)
            pl.wait("scalar", S["xw"], xw_entry)
            dump = lnT[:, 0:D]
            for tt in range(NT):
                pl.op("scalar", lambda e, t=tt: e.activation(
                    dump, xr[:, t * D:(t + 1) * D], AF.Square,
                    accum_out=fsq[:, t:t + 1]),
                    inc=(S["stat"], 1))
            pl.wait("vector", S["stat"], S["stat"].n)
            pl.op("vector", lambda e: e.tensor_scalar_mul(fmu[:], fsum[:], 1.0 / D))
            pl.op("vector", lambda e: e.tensor_scalar_mul(fvar[:], fsq[:], 1.0 / D))
            pl.raw("vector", lambda e: e.drain())
            pl.op("vector", lambda e: e.tensor_mul(ftmp[:], fmu[:], fmu[:]))
            pl.raw("vector", lambda e: e.drain())
            pl.op("vector", lambda e: e.tensor_sub(fvar[:], fvar[:], ftmp[:]))
            pl.raw("vector", lambda e: e.drain())
            pl.op("vector", lambda e: e.tensor_scalar_add(ftmp[:], fvar[:], EPS),
                  inc=(S["stat"], 1))
            pl.wait("scalar", S["stat"], S["stat"].n)
            pl.op("scalar", lambda e: e.activation(ftmp[:], ftmp[:], AF.Sqrt),
                  inc=(S["stat"], 1))
            pl.wait("vector", S["stat"], S["stat"].n)
            pl.op("vector", lambda e: e.reciprocal(frstd[:], ftmp[:]))
            pl.raw("vector", lambda e: e.drain())
            pl.op("vector", lambda e: e.tensor_mul(fnmu[:], fmu[:], frstd[:]))
            pl.raw("vector", lambda e: e.drain())
            pl.op("vector", lambda e: e.tensor_scalar_mul(fnmu[:], fnmu[:], -1.0),
                  inc=(S["stat"], 1))
            stats_done = S["stat"].n

            ln1tok = state["ln1tok"]

            def prep(tt):
                if tt == 0:
                    pl.wait("scalar", S["stat"], stats_done)
                if tt >= 2:
                    pl.wait("scalar", S["trp"], C["tr"] - KT)
                pl.op("scalar", lambda e, t=tt: e.activation(
                    ln1tok[:, (t % 2) * D:((t % 2) + 1) * D],
                    xr[:, t * D:(t + 1) * D], AF.Identity,
                    bias=fnmu[:, t:t + 1], scale=frstd[:, t:t + 1]),
                    inc=(S["lntok"], 1))
                return S["lntok"], S["lntok"].n

            def srcf(tt, kt):
                return ln1tok[:, (tt % 2) * D + kt * P:(tt % 2) * D + (kt + 1) * P]

            tre_base = emit_transposes(srcf, NT, prep, xw_entry, rcp_entry)

            pl.wait("tensor", S["gld"], GLD0)
            pl.wait("tensor", S["tre"], tre_base + NT * KT)
            for tt in range(NT):
                for kt in range(KT):
                    pl.op("tensor", lambda e, t=tt, k=kt: e.matmul(
                        pD[:, t * 8:(t + 1) * 8],
                        lnT[:, k * T + t * P: k * T + (t + 1) * P],
                        gate_s[:, k * 8:(k + 1) * 8],
                        start=(k == 0), stop=(k == KT - 1)),
                        inc=(S["rt"], 1) if kt == KT - 1 else None)
            pl.wait("vector", S["rt"], S["rt"].n)
            pl.op("vector", lambda e: e.tensor_copy(lg_s[:], pD[:, 0:NT * 8]),
                  inc=(S["rcp"], 1))
            pl.raw("vector", lambda e: e.drain())
            lg3 = lg_s[:].rearrange("p (a f) -> p a f", a=NT)
            pl.op("vector", lambda e: e.reduce_max(m1_s[:], lg3, axis=AXX))
            pl.raw("vector", lambda e: e.drain())
            for tt in range(NT):
                pl.op("vector", lambda e, t=tt: e.tensor_scalar(
                    eq1_s[:, t * 8:(t + 1) * 8], lg_s[:, t * 8:(t + 1) * 8],
                    m1_s[:, t:t + 1], None, ALU.is_equal))
            pl.raw("vector", lambda e: e.drain())
            for tt in range(NT):
                pl.op("vector", lambda e, t=tt: e.scalar_tensor_tensor(
                    msk_s[:, t * 8:(t + 1) * 8], eq1_s[:, t * 8:(t + 1) * 8],
                    -1e30, lg_s[:, t * 8:(t + 1) * 8], ALU.mult, ALU.add))
            pl.raw("vector", lambda e: e.drain())
            msk3 = msk_s[:].rearrange("p (a f) -> p a f", a=NT)
            pl.op("vector", lambda e: e.reduce_max(m2_s[:], msk3, axis=AXX))
            pl.raw("vector", lambda e: e.drain())
            pl.op("vector", lambda e: e.tensor_sub(e2_s[:], m2_s[:], m1_s[:]),
                  inc=(S["stat"], 1))
            pl.wait("scalar", S["stat"], S["stat"].n)
            pl.op("scalar", lambda e: e.activation(e2_s[:], e2_s[:], AF.Exp),
                  inc=(S["stat"], 1))
            pl.wait("vector", S["stat"], S["stat"].n)
            pl.op("vector", lambda e: e.tensor_scalar_add(w1_s[:], e2_s[:], 1.0))
            pl.raw("vector", lambda e: e.drain())
            pl.op("vector", lambda e: e.reciprocal(w1_s[:], w1_s[:]))
            pl.raw("vector", lambda e: e.drain())
            pl.op("vector", lambda e: e.tensor_scalar_mul(w1_s[:], w1_s[:], 0.5))
            pl.raw("vector", lambda e: e.drain())
            pl.op("vector", lambda e: e.tensor_mul(w2_s[:], e2_s[:], w1_s[:]))
            pl.raw("vector", lambda e: e.drain())
            for tt in range(NT):
                pl.op("vector", lambda e, t=tt: e.tensor_scalar(
                    msk_s[:, t * 8:(t + 1) * 8], msk_s[:, t * 8:(t + 1) * 8],
                    m2_s[:, t:t + 1], None, ALU.is_equal))
            pl.raw("vector", lambda e: e.drain())
            for tt in range(NT):
                pl.op("vector", lambda e, t=tt: e.tensor_scalar(
                    msk_s[:, t * 8:(t + 1) * 8], msk_s[:, t * 8:(t + 1) * 8],
                    w2_s[:, t:t + 1], None, ALU.mult))
            pl.raw("vector", lambda e: e.drain())
            for tt in range(NT):
                pl.op("vector", lambda e, t=tt: e.scalar_tensor_tensor(
                    comb[:, t * 8:(t + 1) * 8], eq1_s[:, t * 8:(t + 1) * 8],
                    w1_s[:, t:t + 1], msk_s[:, t * 8:(t + 1) * 8],
                    ALU.mult, ALU.add))
            pl.raw("vector", lambda e: e.drain())

            h = state["h"]
            wd = state["wd"]
            for e_i in range(n_experts):
                shared = (e_i == NE - 1)
                for i in range(IT):
                    for (dsrc, buf, cnt, sld, slf) in (
                            (dwg, wg2, "gsl", "wg", "pg"),
                            (dwu, wu2, "usl", "wu", "pu")):
                        sl = C[cnt]
                        if sl >= 2:
                            pl.wait("gpsimd", S[slf], 2 * (sl - 2) + 2)
                        src = dsrc[e_i][:, i * P:(i + 1) * P].rearrange(
                            "(a p) i -> p a i", p=P)
                        dst = buf[:, (sl % 2) * KT * P:((sl % 2) + 1) * KT * P] \
                            .rearrange("p (a i) -> p a i", a=KT)
                        pl.op("gpsimd", lambda e, d_=dst, s_=src:
                              e.dma_start(out=d_, in_=s_), inc=(S[sld], 16))
                        C[cnt] += 1
                if C["ne"] >= 1:
                    pl.wait("gpsimd", S["pd"], 32 * C["ne"])
                if ph == 1 and e_i == 0:
                    pl.wait("gpsimd", S["xw"], xw_entry)
                for i in range(IT):
                    src = dwd[e_i][i * P:(i + 1) * P, :]
                    dst = wd[:, i * D:(i + 1) * D]
                    pl.op("gpsimd", lambda e, d_=dst, s_=src:
                          e.dma_start(out=d_, in_=s_), inc=(S["wd"], 16))
                    C["wsl"] += 1

                for i in range(IT):
                    for tc in range(2):
                        ug = C["ug"]
                        gsl_idx = C["ne"] * IT + i
                        pl.wait("tensor", S["wg"], 16 * (gsl_idx + 1))
                        if ug >= 2:
                            pl.wait("tensor", S["sg"], ug - 1)
                        if i == 0 and e_i == 0:
                            pl.wait("tensor", S["tre"],
                                    tre_base + (4 if tc == 0 else 8) * KT)
                        gslot = pA[:, (ug % 2) * 512:((ug % 2) + 1) * 512]
                        for kt in range(KT):
                            pl.op("tensor", lambda e, k=kt, t=tc, sl=gslot,
                                  b=(gsl_idx % 2): e.matmul(
                                      sl,
                                      wg2[:, b * KT * P + k * P:
                                          b * KT * P + (k + 1) * P],
                                      lnT[:, k * T + t * 512:
                                          k * T + (t + 1) * 512],
                                      start=(k == 0), stop=(k == KT - 1)),
                                  inc=(S["pg"], 1) if kt == KT - 1 else None)
                        pl.wait("tensor", S["wu"], 16 * (gsl_idx + 1))
                        if ug >= 2:
                            pl.wait("tensor", S["h"], ug - 1)
                        uslot = pA[:, 1024 + (ug % 2) * 512:
                                   1024 + ((ug % 2) + 1) * 512]
                        for kt in range(KT):
                            pl.op("tensor", lambda e, k=kt, t=tc, sl=uslot,
                                  b=(gsl_idx % 2): e.matmul(
                                      sl,
                                      wu2[:, b * KT * P + k * P:
                                          b * KT * P + (k + 1) * P],
                                      lnT[:, k * T + t * 512:
                                          k * T + (t + 1) * 512],
                                      start=(k == 0), stop=(k == KT - 1)),
                                  inc=(S["pu"], 1) if kt == KT - 1 else None)
                        pl.wait("scalar", S["pg"], ug + 1)
                        if ug >= 2:
                            pl.wait("scalar", S["h"], ug - 1)
                        sgslot = sg2[:, (ug % 2) * 512:((ug % 2) + 1) * 512]
                        pl.op("scalar", lambda e, sl=sgslot, g=gslot:
                              e.activation(sl, g, AF.Silu), inc=(S["sg"], 1))
                        pl.wait("vector", S["pu"], ug + 1)
                        pl.wait("vector", S["sg"], ug + 1)
                        hdst = h[:, i * T + tc * 512: i * T + (tc + 1) * 512]
                        pl.op("vector", lambda e, d_=hdst, sl=sgslot, u=uslot:
                              e.tensor_mul(d_, sl, u), inc=(S["h"], 1))
                        C["ug"] += 1

                pl.wait("tensor", S["h"], C["ug"])
                pl.wait("tensor", S["wd"], 16 * C["wsl"])
                if e_i == 0:
                    pl.wait("tensor", S["tre"], tre_base + NT * KT)
                for t128 in range(NT):
                    for hh in range(4):
                        ud = C["ud"]
                        if ud >= 2:
                            pl.wait("tensor", S["xw"], state["stt_val"][ud - 2])
                        dslot = pB[:, (ud % 2) * 512:((ud % 2) + 1) * 512]
                        for i in range(IT):
                            pl.op("tensor", lambda e, t=t128, q=hh, ii=i, sl=dslot:
                                  e.matmul(
                                      sl,
                                      h[:, ii * T + t * P: ii * T + (t + 1) * P],
                                      wd[:, ii * D + q * 512:
                                         ii * D + (q + 1) * 512],
                                      start=(ii == 0), stop=(ii == IT - 1)),
                                  inc=(S["pd"], 1) if i == IT - 1 else None)
                        pl.wait("vector", S["pd"], S["pd"].n)
                        xv = xr[:, t128 * D + hh * 512: t128 * D + (hh + 1) * 512]
                        if shared:
                            v = pl.op("vector", lambda e, x=xv, sl=dslot:
                                      e.scalar_tensor_tensor(
                                          x, sl, 0.5, x, ALU.mult, ALU.add),
                                      inc=(S["xw"], 1))
                        else:
                            cc = comb[:, t128 * 8 + e_i: t128 * 8 + e_i + 1]
                            v = pl.op("vector", lambda e, x=xv, sl=dslot, c=cc:
                                      e.scalar_tensor_tensor(
                                          x, sl, c, x, ALU.mult, ALU.add),
                                      inc=(S["xw"], 1))
                        state["stt_val"].append(v)
                        if e_i == n_experts - 1 and hh == 3:
                            state["tile_last_stt"][(ph, t128)] = v
                        C["ud"] += 1
                C["ne"] += 1

        g_h1 = nc.sbuf_tensor("h1", [P, IT * T], BF16)
        g_tok1 = nc.sbuf_tensor("tok1", [P, 2 * D], BF16)
        g_wd1 = nc.sbuf_tensor("wd1", [P, IT * D], BF16)
        state["h"] = g_h1.__enter__()
        state["ln1tok"] = g_tok1.__enter__()
        state["wd"] = g_wd1.__enter__()
        ffn_call(0, xw_entry=0, rcp_entry=0)
        F1_XW = S["xw"].n
        g_wd1.__exit__(None, None, None)
        g_tok1.__exit__(None, None, None)
        g_h1.__exit__(None, None, None)
        mamba_xw = F1_XW

        if do_mamba:
            m_guards = []

            def msb(name, shape, dt):
                g = nc.sbuf_tensor(name, list(shape), dt)
                m_guards.append(g)
                return g.__enter__()

            zz = msb("zz", [P, 2 * D], BF16)
            wbc = msb("wbc", [P, D], BF16)
            bbc = msb("bbc", [P, D], BF16)
            xin = msb("xin", [DIN, T + 2], F32)
            u_s = msb("u_s", [DIN, T], F32)
            delta = msb("delta", [DIN, T], F32)
            dA0 = msb("dA0", [DIN, T], F32)
            dA1 = msb("dA1", [DIN, T], F32)
            dBu = msb("dBu", [DIN, T], F32)
            xsc = msb("xsc", [DIN, T], F32)
            y_s = msb("y_s", [DIN, T], F32)
            Bp_s = msb("Bp_s", [DST, T], F32)
            Cp_s = msb("Cp_s", [DST, T], F32)
            y16 = msb("y16", [DIN, T], BF16)
            bcB = msb("bcB", [DIN, 2 * T], F32)
            bcC = msb("bcC", [DIN, 2 * T], F32)
            dAbufs = [dA0, dA1]

            pl.wait("gpsimd", S["xw"], F1_XW)
            pl.op("gpsimd", lambda e: e.dma_start(out=wbc[:], in_=dlnw),
                  inc=(S["gld"], 16))
            pl.op("gpsimd", lambda e: e.dma_start(out=bbc[:], in_=dlnb),
                  inc=(S["gld"], 16))
            GLDW = S["gld"].n

            for tt in range(NT):
                pl.op("vector", lambda e, t=tt: e.reduce_sum(
                    fsum[:, t:t + 1], xr[:, t * D:(t + 1) * D], axis=AXX),
                    inc=(S["mA"], 1))
            pl.wait("scalar", S["xw"], F1_XW)
            for tt in range(NT):
                pl.op("scalar", lambda e, t=tt: e.activation(
                    zz[:, 0:D], xr[:, t * D:(t + 1) * D], AF.Square,
                    accum_out=fsq[:, t:t + 1]),
                    inc=(S["mA"], 1))
            pl.wait("vector", S["mA"], S["mA"].n)
            pl.op("vector", lambda e: e.tensor_scalar_mul(mmu[:], fsum[:], 1.0 / D))
            pl.op("vector", lambda e: e.tensor_scalar_mul(mvar[:], fsq[:], 1.0 / D))
            pl.raw("vector", lambda e: e.drain())
            pl.op("vector", lambda e: e.tensor_mul(mtmp[:], mmu[:], mmu[:]))
            pl.raw("vector", lambda e: e.drain())
            pl.op("vector", lambda e: e.tensor_sub(mvar[:], mvar[:], mtmp[:]))
            pl.raw("vector", lambda e: e.drain())
            pl.op("vector", lambda e: e.tensor_scalar_add(mtmp[:], mvar[:], EPS),
                  inc=(S["mB"], 1))
            pl.wait("scalar", S["mB"], S["mB"].n)
            pl.op("scalar", lambda e: e.activation(mtmp[:], mtmp[:], AF.Sqrt),
                  inc=(S["mB"], 1))
            pl.wait("vector", S["mB"], S["mB"].n)
            pl.op("vector", lambda e: e.reciprocal(mrstd[:], mtmp[:]))
            pl.raw("vector", lambda e: e.drain())
            pl.op("vector", lambda e: e.tensor_mul(mtmp[:], mvar[:], mrstd[:]))
            pl.raw("vector", lambda e: e.drain())
            pl.op("vector", lambda e: e.tensor_mul(mtmp[:], mtmp[:], mrstd[:]))
            pl.raw("vector", lambda e: e.drain())
            pl.op("vector", lambda e: e.tensor_scalar_add(mtmp[:], mtmp[:], EPS),
                  inc=(S["mB"], 1))
            pl.wait("scalar", S["mB"], S["mB"].n)
            pl.op("scalar", lambda e: e.activation(mtmp[:], mtmp[:], AF.Sqrt),
                  inc=(S["mB"], 1))
            pl.wait("vector", S["mB"], S["mB"].n)
            pl.op("vector", lambda e: e.reciprocal(mtmp[:], mtmp[:]))
            pl.raw("vector", lambda e: e.drain())
            pl.op("vector", lambda e: e.tensor_mul(mseff[:], mrstd[:], mtmp[:]))
            pl.raw("vector", lambda e: e.drain())
            pl.op("vector", lambda e: e.tensor_mul(mnb[:], mmu[:], mseff[:]))
            pl.op("vector", lambda e: e.tensor_mul(mnmr[:], mmu[:], mrstd[:]))
            pl.raw("vector", lambda e: e.drain())
            pl.op("vector", lambda e: e.tensor_scalar_mul(mnb[:], mnb[:], -1.0))
            pl.op("vector", lambda e: e.tensor_scalar_mul(mnmr[:], mnmr[:], -1.0),
                  inc=(S["mB"], 1))
            mstats = S["mB"].n

            def mprep(tt):
                if tt == 0:
                    pl.wait("scalar", S["mB"], mstats)
                    pl.wait("scalar", S["gld"], GLDW)
                if tt >= 2:
                    pl.wait("scalar", S["trp"], C["tr"] - KT)
                pl.op("scalar", lambda e, t=tt: e.activation(
                    zz[:, (t % 2) * D:((t % 2) + 1) * D],
                    xr[:, t * D:(t + 1) * D], AF.Identity,
                    bias=mnb[:, t:t + 1], scale=mseff[:, t:t + 1]),
                    inc=(S["mC"], 1))
                pl.wait("vector", S["mC"], S["mC"].n)
                zv = zz[:, (tt % 2) * D:((tt % 2) + 1) * D]
                pl.op("vector", lambda e, z=zv: e.tensor_mul(z, z, wbc[:]))
                pl.op("vector", lambda e, z=zv: e.tensor_add(z, z, bbc[:]),
                      inc=(S["mC"], 1))
                return S["mC"], S["mC"].n

            def msrc(tt, kt):
                return zz[:, (tt % 2) * D + kt * P:(tt % 2) * D + (kt + 1) * P]

            tre_base2 = emit_transposes(msrc, NT, mprep, F1_XW, 1)

            pl.wait("tensor", S["tre"], tre_base2 + NT * KT)
            pl.wait("tensor", S["gld"], GLD0)
            for tc in range(2):
                for kt in range(KT):
                    pl.op("tensor", lambda e, k=kt, t=tc: e.matmul(
                        pA[0:DIN, t * 512:(t + 1) * 512],
                        inp_s[:, k * DIN:(k + 1) * DIN],
                        lnT[:, k * T + t * 512: k * T + (t + 1) * 512],
                        start=(k == 0), stop=(k == KT - 1)),
                        inc=(S["mD"], 1) if (kt == KT - 1) else None)
            pl.wait("scalar", S["mD"], S["mD"].n)
            pl.op("scalar", lambda e: e.copy(xin[:, 1:T + 1], pA[0:DIN, 0:T]),
                  inc=(S["mD"], 1))
            inproj_done = S["mD"].n

            pl.wait("sync", S["mD"], inproj_done)
            pl.op("sync", lambda e: e.dma_start(
                out=dag1i[:, 0:1], in_=xin[:, 1:2],
                allow_slow_non_contiguous=True), inc=(S["ag"], 16))
            pl.op("sync", lambda e: e.dma_start(
                out=dag1i[:, 1:2], in_=xin[:, T:T + 1],
                allow_slow_non_contiguous=True), inc=(S["ag"], 16))
            pl.wait("gpsimd", S["ag"], S["ag"].n)
            pl.op("gpsimd", lambda e: e.collective_compute(
                "AllGather", ALU.bypass, replica_groups=groups,
                ins=[dag1i], outs=[dag1o]), inc=(S["ag"], 1))
            pl.wait("gpsimd", S["ag"], S["ag"].n)
            pl.op("gpsimd", lambda e: e.dma_start(
                out=slots1[:].rearrange("p (r f) -> p r f", r=4),
                in_=dag1o.rearrange("(r p) f -> p r f", r=4)),
                inc=(S["ag"], 16))
            pl.wait("vector", S["ag"], S["ag"].n)
            pl.op("vector", lambda e: e.memset(xin[:, 0:1], 0.0))
            pl.op("vector", lambda e: e.memset(xin[:, T + 1:T + 2], 0.0))
            pl.raw("vector", lambda e: e.drain())
            sl1 = slots1[:].rearrange("p (r f) -> p r f", r=4)
            for r in range(4):
                pl.op("vector", lambda e, rr=r: e.scalar_tensor_tensor(
                    xin[:, 0:1], sl1[:, rr, 1:2], selL_s[:, rr:rr + 1],
                    xin[:, 0:1], ALU.mult, ALU.add))
                pl.raw("vector", lambda e: e.drain())
                pl.op("vector", lambda e, rr=r: e.scalar_tensor_tensor(
                    xin[:, T + 1:T + 2], sl1[:, rr, 0:1], selR_s[:, rr:rr + 1],
                    xin[:, T + 1:T + 2], ALU.mult, ALU.add))
                pl.raw("vector", lambda e: e.drain())

            pl.op("vector", lambda e: e.tensor_scalar(
                delta[:], xin[:, 0:T], convw_s[:, 0:1], None, ALU.mult))
            pl.op("vector", lambda e: e.scalar_tensor_tensor(
                delta[:], xin[:, 1:T + 1], convw_s[:, 1:2], delta[:],
                ALU.mult, ALU.add))
            pl.op("vector", lambda e: e.scalar_tensor_tensor(
                delta[:], xin[:, 2:T + 2], convw_s[:, 2:3], delta[:],
                ALU.mult, ALU.add), inc=(S["mD"], 1))
            pl.wait("scalar", S["mD"], S["mD"].n)
            pl.op("scalar", lambda e: e.activation(
                u_s[:], delta[:], AF.Silu, bias=convb_s[:]), inc=(S["mD"], 1))
            u_done = S["mD"].n

            pl.wait("tensor", S["mD"], u_done)
            for tc in range(2):
                pl.op("tensor", lambda e, t=tc: e.matmul(
                    pA[0:DST, 1024 + t * 512:1024 + (t + 1) * 512],
                    xpB_s[:], u_s[:, t * 512:(t + 1) * 512],
                    start=True, stop=True),
                    inc=(S["mE"], 1) if tc == 1 else None)
                pl.op("tensor", lambda e, t=tc: e.matmul(
                    pB[0:DST, t * 512:(t + 1) * 512],
                    xpC_s[:], u_s[:, t * 512:(t + 1) * 512],
                    start=True, stop=True),
                    inc=(S["mE"], 1) if tc == 1 else None)
            pl.wait("scalar", S["mE"], S["mE"].n)
            pl.op("scalar", lambda e: e.copy(Bp_s[:], pA[0:DST, 1024:2048]),
                  inc=(S["mE"], 1))
            pl.op("scalar", lambda e: e.copy(Cp_s[:], pB[0:DST, 0:T]),
                  inc=(S["mE"], 1))
            bc_evac = S["mE"].n
            pl.wait("tensor", S["mE"], bc_evac)
            for tc in range(2):
                pl.op("tensor", lambda e, t=tc: e.matmul(
                    pA[0:DTR, 1024 + t * 512:1024 + (t + 1) * 512],
                    xpd_s[:], u_s[:, t * 512:(t + 1) * 512],
                    start=True, stop=True),
                    inc=(S["mE"], 1) if tc == 1 else None)
            pl.wait("scalar", S["mE"], S["mE"].n)
            pl.op("scalar", lambda e: e.copy(xsc[0:DTR, :], pA[0:DTR, 1024:2048]),
                  inc=(S["mE"], 1))
            dr_evac = S["mE"].n
            pl.wait("tensor", S["mE"], dr_evac)
            for tc in range(2):
                pl.op("tensor", lambda e, t=tc: e.matmul(
                    pA[0:DIN, t * 512:(t + 1) * 512],
                    dtT_s[:], xsc[0:DTR, t * 512:(t + 1) * 512],
                    start=True, stop=True),
                    inc=(S["mE"], 1) if tc == 1 else None)
            pl.wait("scalar", S["mE"], S["mE"].n)
            pl.op("scalar", lambda e: e.activation(
                dA0[:], pA[0:DIN, 0:T], AF.Abs, bias=dtb_s[:]))
            pl.op("scalar", lambda e: e.activation(
                dA1[:], dA0[:], AF.Exp, scale=-1.0))
            pl.op("scalar", lambda e: e.activation(
                dA1[:], dA1[:], AF.Ln, bias=1.0))
            pl.op("scalar", lambda e: e.activation(
                dA0[:], pA[0:DIN, 0:T], AF.Relu, bias=dtb_s[:]),
                inc=(S["mE"], 1))
            pl.wait("vector", S["mE"], S["mE"].n)
            pl.op("vector", lambda e: e.tensor_add(delta[:], dA0[:], dA1[:]))
            pl.op("vector", lambda e: e.tensor_scalar_mul(delta[:], delta[:],
                                                          0.001))
            pl.raw("vector", lambda e: e.drain())
            pl.op("vector", lambda e: e.reduce_sum(Stot_s[:], delta[:], axis=AXX))
            pl.raw("vector", lambda e: e.drain())
            pl.op("vector", lambda e: e.tensor_scalar(
                Atot_s[:], negA[:], Stot_s[:], None, ALU.mult), inc=(S["mF"], 1))
            pl.wait("scalar", S["mF"], S["mF"].n)
            pl.op("scalar", lambda e: e.activation(Atot_s[:], Atot_s[:], AF.Exp),
                  inc=(S["mF"], 1))
            prep_done = S["mF"].n

            pl.wait("sync", S["mE"], bc_evac)
            pl.op("sync", lambda e: e.dma_start(out=dbpcp[0:DST, :], in_=Bp_s[:]),
                  inc=(S["bcl"], 16))
            pl.op("sync", lambda e: e.dma_start(out=dbpcp[DST:2 * DST, :],
                                                in_=Cp_s[:]),
                  inc=(S["bcl"], 16))

            nbc = 0
            for n in range(DST):
                if nbc >= 2:
                    pl.wait("sync", S["bcf"], nbc - 1)
                bslot = bcB[:, (n % 2) * T:((n % 2) + 1) * T]
                pl.op("sync", lambda e, d_=bslot, r=n: e.dma_start(
                    out=d_, in_=dbpcp[r:r + 1, :].broadcast_to([DIN, T])),
                    inc=(S["bcl"], 16))
                nbc += 1
                if n >= 2:
                    pl.wait("scalar", S["mG"], 3 * (n - 2) + 2)
                pl.op("scalar", lambda e, r=n, b=dAbufs[n % 2]: e.activation(
                    b[:], delta[:], AF.Exp, scale=negA[:, r:r + 1]),
                    inc=(S["mF"], 1))
                pl.wait("vector", S["mF"], prep_done + n + 1)
                pl.op("vector", lambda e: e.tensor_mul(dBu[:], delta[:], u_s[:]))
                pl.wait("vector", S["bcl"], 32 + 16 * nbc)
                pl.op("vector", lambda e, b=bslot: e.tensor_mul(dBu[:], dBu[:], b),
                      inc=(S["bcf"], 1), inc2=(S["mG"], 1))
                pl.op("vector", lambda e, b=dAbufs[n % 2]: e.tensor_tensor_scan(
                    xsc[:], b[:], dBu[:], 0.0, ALU.mult, ALU.add),
                    inc=(S["mG"], 1))
                pl.raw("vector", lambda e: e.drain())
                pl.op("vector", lambda e, r=n: e.tensor_copy(
                    F_s[:, r:r + 1], xsc[:, T - 1:T]), inc=(S["mG"], 1))
                pl.raw("vector", lambda e: e.drain())
            pass1_done = S["mG"].n

            pl.wait("sync", S["mG"], pass1_done)
            pl.op("sync", lambda e: e.dma_start(out=dag2i[:, 0:DST], in_=F_s[:]),
                  inc=(S["ag"], 16))
            pl.op("sync", lambda e: e.dma_start(out=dag2i[:, DST:2 * DST],
                                                in_=Atot_s[:]),
                  inc=(S["ag"], 16))
            pl.wait("gpsimd", S["ag"], S["ag"].n)
            pl.op("gpsimd", lambda e: e.collective_compute(
                "AllGather", ALU.bypass, replica_groups=groups,
                ins=[dag2i], outs=[dag2o]), inc=(S["ag"], 1))
            pl.wait("gpsimd", S["ag"], S["ag"].n)
            pl.op("gpsimd", lambda e: e.dma_start(
                out=slots2[:].rearrange("p (r f) -> p r f", r=4),
                in_=dag2o.rearrange("(r p) f -> p r f", r=4)),
                inc=(S["ag"], 16))
            pl.wait("vector", S["ag"], S["ag"].n)
            sl2 = slots2[:].rearrange("p (r f) -> p r f", r=4)
            pl.op("vector", lambda e: e.tensor_copy(cs1[:], sl2[:, 0, 0:DST]))
            pl.raw("vector", lambda e: e.drain())
            pl.op("vector", lambda e: e.tensor_mul(ctmp[:],
                                                   sl2[:, 1, DST:2 * DST], cs1[:]))
            pl.raw("vector", lambda e: e.drain())
            pl.op("vector", lambda e: e.tensor_add(cs2[:], ctmp[:],
                                                   sl2[:, 1, 0:DST]))
            pl.raw("vector", lambda e: e.drain())
            pl.op("vector", lambda e: e.tensor_mul(ctmp[:],
                                                   sl2[:, 2, DST:2 * DST], cs2[:]))
            pl.raw("vector", lambda e: e.drain())
            pl.op("vector", lambda e: e.tensor_add(cs3[:], ctmp[:],
                                                   sl2[:, 2, 0:DST]))
            pl.raw("vector", lambda e: e.drain())
            pl.op("vector", lambda e: e.memset(carry[:], 0.0))
            pl.raw("vector", lambda e: e.drain())
            for j, csj in enumerate((cs1, cs2, cs3)):
                pl.op("vector", lambda e, c=csj, jj=j: e.scalar_tensor_tensor(
                    carry[:], c[:], selc_s[:, jj:jj + 1], carry[:],
                    ALU.mult, ALU.add))
                pl.raw("vector", lambda e: e.drain())
            pl.op("vector", lambda e: e.memset(y_s[:], 0.0), inc=(S["mG"], 1))

            for n in range(DST):
                if nbc >= 2:
                    pl.wait("sync", S["bcf"], nbc - 1)
                bslot = bcB[:, (n % 2) * T:((n % 2) + 1) * T]
                pl.op("sync", lambda e, d_=bslot, r=n: e.dma_start(
                    out=d_, in_=dbpcp[r:r + 1, :].broadcast_to([DIN, T])),
                    inc=(S["bcl"], 16))
                nbc += 1
                if nbc >= 2:
                    pl.wait("sync", S["bcf"], nbc - 1)
                cslot = bcC[:, (n % 2) * T:((n % 2) + 1) * T]
                pl.op("sync", lambda e, d_=cslot, r=n: e.dma_start(
                    out=d_,
                    in_=dbpcp[DST + r:DST + r + 1, :].broadcast_to([DIN, T])),
                    inc=(S["bcl"], 16))
                nbc += 1
                if n >= 2:
                    pl.wait("scalar", S["mH"], 4 * (n - 2) + 2)
                pl.op("scalar", lambda e, r=n, b=dAbufs[n % 2]: e.activation(
                    b[:], delta[:], AF.Exp, scale=negA[:, r:r + 1]),
                    inc=(S["mF"], 1))
                pl.wait("vector", S["mF"], prep_done + DST + n + 1)
                pl.op("vector", lambda e: e.tensor_mul(dBu[:], delta[:], u_s[:]))
                pl.wait("vector", S["bcl"], 32 + 16 * nbc)
                pl.op("vector", lambda e, b=bslot: e.tensor_mul(dBu[:], dBu[:], b),
                      inc=(S["bcf"], 1), inc2=(S["mH"], 1))
                pl.op("vector", lambda e, b=dAbufs[n % 2], r=n:
                      e.tensor_tensor_scan(
                          xsc[:], b[:], dBu[:], carry[:, r:r + 1],
                          ALU.mult, ALU.add),
                      inc=(S["mH"], 1))
                pl.op("vector", lambda e, c=cslot: e.tensor_mul(dBu[:], xsc[:], c),
                      inc=(S["bcf"], 1), inc2=(S["mH"], 1))
                pl.op("vector", lambda e: e.tensor_add(y_s[:], y_s[:], dBu[:]),
                      inc=(S["mH"], 1))
            pl.op("vector", lambda e: e.scalar_tensor_tensor(
                y_s[:], u_s[:], Dp_s[:], y_s[:], ALU.mult, ALU.add))
            pl.op("vector", lambda e: e.tensor_copy(y16[:], y_s[:]),
                  inc=(S["mH"], 1))
            y_done = S["mH"].n

            pl.wait("tensor", S["mH"], y_done)
            for tt in range(NT):
                for q in range(4):
                    pl.op("tensor", lambda e, t=tt, qq=q: e.matmul(
                        pA[:, qq * 512:(qq + 1) * 512],
                        y16[:, t * P:(t + 1) * P],
                        outT_s[:, qq * 512:(qq + 1) * 512],
                        start=True, stop=True),
                        inc=(S["mH"], 1) if q == 3 else None)
                pl.wait("vector", S["mH"], S["mH"].n)
                xv = xr[:, tt * D:(tt + 1) * D]
                pl.op("vector", lambda e, x=xv, t=tt: e.scalar_tensor_tensor(
                    x, x, mrstd[:, t:t + 1], x, ALU.mult, ALU.add))
                pl.op("vector", lambda e, x=xv, t=tt: e.tensor_scalar(
                    x, x, mnmr[:, t:t + 1], None, ALU.add))
                vres = pl.op("vector", lambda e, x=xv: e.tensor_add(
                    x, x, pA[:, 0:D]), inc=(S["xw"], 1))
                pl.wait("tensor", S["xw"], vres)
            for g in reversed(m_guards):
                g.__exit__(None, None, None)
            mamba_xw = S["xw"].n

        if do_ffn2:
            g_h2 = nc.sbuf_tensor("h2", [P, IT * T], BF16)
            g_tok2 = nc.sbuf_tensor("tok2", [P, 2 * D], BF16)
            g_wd2 = nc.sbuf_tensor("wd2", [P, IT * D], BF16)
            state["h"] = g_h2.__enter__()
            state["ln1tok"] = g_tok2.__enter__()
            state["wd"] = g_wd2.__enter__()
            ffn_call(1, xw_entry=mamba_xw, rcp_entry=1)
            g_wd2.__exit__(None, None, None)
            g_tok2.__exit__(None, None, None)
            g_h2.__exit__(None, None, None)

        if do_ffn2:
            for tt in range(NT):
                pl.wait("sync", S["xw"], state["tile_last_stt"][(1, tt)])
                pl.op("sync", lambda e, t=tt: e.dma_start(
                    out=dy[t * P:(t + 1) * P, :], in_=xr[:, t * D:(t + 1) * D]),
                    inc=(S["st"], 16))
        else:
            pl.wait("sync", S["xw"], S["xw"].n)
            for tt in range(NT):
                pl.op("sync", lambda e, t=tt: e.dma_start(
                    out=dy[t * P:(t + 1) * P, :], in_=xr[:, t * D:(t + 1) * D]),
                    inc=(S["st"], 16))
        pl.wait("sync", S["st"], S["st"].n)

        with nc.Block() as block:
            block.tensor(lambda e: [f(e) for f in pl.q["tensor"]])
            block.vector(lambda e: [f(e) for f in pl.q["vector"]])
            block.scalar(lambda e: [f(e) for f in pl.q["scalar"]])
            block.sync(lambda e: [f(e) for f in pl.q["sync"]])
            block.gpsimd(lambda e: [f(e) for f in pl.q["gpsimd"]])

    return nc


def prep_inputs(inputs):
    f = np.ascontiguousarray
    w = {}
    eg = inputs["L0_ffn_expert_gate_w"]
    eu = inputs["L0_ffn_expert_up_w"]
    ed = inputs["L0_ffn_expert_down_w"]
    sg = inputs["L0_ffn_shared_gate_w"]
    su = inputs["L0_ffn_shared_up_w"]
    sd = inputs["L0_ffn_shared_down_w"]
    w["wgT"] = f(np.concatenate([eg.transpose(0, 2, 1), sg.T[None]], axis=0))
    w["wuT"] = f(np.concatenate([eu.transpose(0, 2, 1), su.T[None]], axis=0))
    w["wdT"] = f(np.concatenate([ed.transpose(0, 2, 1), sd.T[None]], axis=0))
    w["gateT"] = f(inputs["L0_ffn_gate_weight"].T)
    w["inpT"] = f(inputs["L0_attn_in_proj_W"].T)
    xp = inputs["L0_attn_x_proj_W"]
    w["xpd"] = f(xp[0:DTR].T)
    w["xpB"] = f(xp[DTR:DTR + DST].T)
    w["xpC"] = f(xp[DTR + DST:DTR + 2 * DST].T)
    w["dtT"] = f(inputs["L0_attn_dt_proj_W"].T)
    w["dtb"] = f(inputs["L0_attn_dt_proj_b"].reshape(DIN, 1))
    w["convw"] = f(inputs["L0_attn_conv_w"].reshape(DIN, 3))
    w["convb"] = f(inputs["L0_attn_conv_b"].reshape(DIN, 1))
    w["Alog"] = f(inputs["L0_attn_A_log"])
    w["Dp"] = f(inputs["L0_attn_D_param"].reshape(DIN, 1))
    w["outT"] = f(inputs["L0_attn_out_proj_W"].T)
    w["lnw_bc"] = f(np.broadcast_to(np.asarray(inputs["L0_attn_ln_w"])[None, :],
                                    (P, D)))
    w["lnb_bc"] = f(np.broadcast_to(np.asarray(inputs["L0_attn_ln_b"])[None, :],
                                    (P, D)))
    w["ident"] = np.eye(P, dtype=np.float32)
    for k in w:
        w[k] = np.asarray(w[k], dtype=np.float32)
    return w


def make_in_maps(x, w):
    in_maps = []
    for c in range(NCORES):
        b, ch = c // 4, c % 4
        m = dict(w)
        m["x"] = np.ascontiguousarray(x[b, ch * T:(ch + 1) * T, :])
        selc = np.zeros((DIN, 3), np.float32)
        if ch >= 1:
            selc[:, ch - 1] = 1.0
        selL = np.zeros((DIN, 4), np.float32)
        selR = np.zeros((DIN, 4), np.float32)
        if ch > 0:
            selL[:, ch - 1] = 1.0
        if ch < 3:
            selR[:, ch + 1] = 1.0
        m["selc"] = selc
        m["selL"] = selL
        m["selR"] = selR
        in_maps.append(m)
    return in_maps


def kernel(**inputs):
    x = np.asarray(inputs["L0_attn_feat_patch"], dtype=np.float32)
    w = prep_inputs(inputs)
    nc = build_program()
    in_maps = make_in_maps(x, w)
    res = run_bass_kernel_spmd(nc, in_maps, core_ids=list(range(NCORES)))
    out = np.empty((2, 4096, D), dtype=np.float32)
    for c in range(NCORES):
        b, ch = c // 4, c % 4
        out[b, ch * T:(ch + 1) * T, :] = np.asarray(res.results[c]["y"],
                                                    dtype=np.float32)
    return out


if __name__ == "__main__":
    import reference as _r
    inp = _r.setup_inputs()
    got = kernel(**{k: np.asarray(v) for k, v in inp.items()})
    want = np.asarray(_r.reference(**inp))
    err = np.linalg.norm(got - want) / np.linalg.norm(want)
    print("rel_err", err)


# version 04
# speedup vs baseline: 1.0115x; 1.0115x over previous
import contextlib
import numpy as np
import concourse.bass as bass
import concourse.mybir as mybir
from concourse.bass_utils import run_bass_kernel_spmd

F32 = mybir.dt.float32
BF16 = mybir.dt.bfloat16
AF = mybir.ActivationFunctionType
ALU = mybir.AluOpType
AXX = mybir.AxisListType.X

P = 128
T = 1024
NT = 8
D = 2048
KT = 16
II = 1408
IT = 11
NE = 9
DIN = 96
DST = 16
DTR = 6
EPS = 1e-5
NCORES = 8

ENGS = ("tensor", "vector", "scalar", "sync", "gpsimd")


class Sem:
    def __init__(self, es, nc, name):
        self.h = es.enter_context(nc.semaphore(name))
        self.n = 0

    def bump(self, k=1):
        self.n += k
        return self.n


class Plan:
    def __init__(self):
        self.q = {k: [] for k in ENGS}

    def wait(self, eng, sem, val):
        if val <= 0:
            return
        self.q[eng].append(lambda e, s=sem.h, v=val: e.wait_ge(s, v))

    def op(self, eng, build, inc=None, inc2=None):
        if inc is None:
            self.q[eng].append(build)
            return None
        sem, k = inc
        v = sem.bump(k)
        self.q[eng].append(lambda e, b=build, s=sem.h, kk=k: b(e).then_inc(s, kk))
        if inc2 is not None:
            sem2, k2 = inc2
            sem2.bump(k2)
            self.q[eng].append(
                lambda e, s2=sem2.h, kk2=k2: e.nop().then_inc(s2, kk2))
        return v

    def raw(self, eng, fn):
        self.q[eng].append(fn)


def build_program(do_mamba=True, do_ffn2=True, n_experts=NE):
    nc = bass.Bass("TRN2", target_bir_lowering=False)

    def din(name, shape):
        return nc.dram_tensor(name, list(shape), F32, kind="ExternalInput").ap()

    dx = din("x", [T, D])
    dwg = din("wgT", [NE, D, II])
    dwu = din("wuT", [NE, D, II])
    dwd = din("wdT", [NE, II, D])
    dgate = din("gateT", [D, 8])
    dinp = din("inpT", [D, DIN])
    dxpB = din("xpB", [DIN, DST])
    dxpC = din("xpC", [DIN, DST])
    dxpd = din("xpd", [DIN, DTR])
    ddtT = din("dtT", [DTR, DIN])
    ddtb = din("dtb", [DIN, 1])
    dconvw = din("convw", [DIN, 3])
    dconvb = din("convb", [DIN, 1])
    dAlog = din("Alog", [DIN, DST])
    dDp = din("Dp", [DIN, 1])
    doutT = din("outT", [DIN, D])
    dlnw = din("lnw_bc", [P, D])
    dlnb = din("lnb_bc", [P, D])
    dident = din("ident", [P, P])
    dselc = din("selc", [DIN, 3])
    dselL = din("selL", [DIN, 4])
    dselR = din("selR", [DIN, 4])
    dy = nc.dram_tensor("y", [T, D], F32, kind="ExternalOutput").ap()
    dag1i = nc.dram_tensor("ag1i", [DIN, 2], F32, kind="Internal").ap()
    dag1o = nc.dram_tensor("ag1o", [4 * DIN, 2], F32, kind="Internal").ap()
    dag2i = nc.dram_tensor("ag2i", [DIN, 2 * DST], F32, kind="Internal").ap()
    dag2o = nc.dram_tensor("ag2o", [4 * DIN, 2 * DST], F32, kind="Internal").ap()
    dbpcp = nc.dram_tensor("bpcp", [2 * DST, T], F32, kind="Internal").ap()
    groups = [[0, 1, 2, 3], [4, 5, 6, 7]]

    pl = Plan()
    es = contextlib.ExitStack()
    with es:
        sb = lambda name, shape, dt: es.enter_context(
            nc.sbuf_tensor(name, list(shape), dt))
        xr = sb("xr", [P, NT * D], F32)
        lnT = sb("lnT", [P, KT * T], BF16)
        wg2 = sb("wg2", [P, 3 * KT * P], BF16)
        wu2 = sb("wu2", [P, 3 * KT * P], BF16)
        ident_s = sb("ident_s", [P, P], BF16)
        gate_s = sb("gate_s", [P, KT * 8], BF16)
        sg2 = sb("sg2", [P, 2 * 512], F32)
        lg_s = sb("lg_s", [P, NT * 8], F32)
        m1_s = sb("m1_s", [P, NT], F32)
        m2_s = sb("m2_s", [P, NT], F32)
        e2_s = sb("e2_s", [P, NT], F32)
        w1_s = sb("w1_s", [P, NT], F32)
        w2_s = sb("w2_s", [P, NT], F32)
        eq1_s = sb("eq1_s", [P, NT * 8], F32)
        msk_s = sb("msk_s", [P, NT * 8], F32)
        comb = sb("comb", [P, NT * 8], F32)
        fsum = sb("fsum", [P, NT], F32)
        fsq = sb("fsq", [P, NT], F32)
        fmu = sb("fmu", [P, NT], F32)
        fvar = sb("fvar", [P, NT], F32)
        ftmp = sb("ftmp", [P, NT], F32)
        frstd = sb("frstd", [P, NT], F32)
        fnmu = sb("fnmu", [P, NT], F32)
        mmu = sb("mmu", [P, NT], F32)
        mvar = sb("mvar", [P, NT], F32)
        mrstd = sb("mrstd", [P, NT], F32)
        mtmp = sb("mtmp", [P, NT], F32)
        mseff = sb("mseff", [P, NT], F32)
        mnb = sb("mnb", [P, NT], F32)
        mnmr = sb("mnmr", [P, NT], F32)
        xpB_s = sb("xpB_s", [DIN, DST], F32)
        xpC_s = sb("xpC_s", [DIN, DST], F32)
        xpd_s = sb("xpd_s", [DIN, DTR], F32)
        dtT_s = sb("dtT_s", [DTR, DIN], F32)
        dtb_s = sb("dtb_s", [DIN, 1], F32)
        convw_s = sb("convw_s", [DIN, 3], F32)
        convb_s = sb("convb_s", [DIN, 1], F32)
        Alog_s = sb("Alog_s", [DIN, DST], F32)
        negA = sb("negA", [DIN, DST], F32)
        Dp_s = sb("Dp_s", [DIN, 1], F32)
        selc_s = sb("selc_s", [DIN, 3], F32)
        selL_s = sb("selL_s", [DIN, 4], F32)
        selR_s = sb("selR_s", [DIN, 4], F32)
        inp_s = sb("inp_s", [P, KT * DIN], BF16)
        outT_s = sb("outT_s", [DIN, D], BF16)
        F_s = sb("F_s", [DIN, DST], F32)
        Atot_s = sb("Atot_s", [DIN, DST], F32)
        carry = sb("carry", [DIN, DST], F32)
        cs1 = sb("cs1", [DIN, DST], F32)
        cs2 = sb("cs2", [DIN, DST], F32)
        cs3 = sb("cs3", [DIN, DST], F32)
        ctmp = sb("ctmp", [DIN, DST], F32)
        slots2 = sb("slots2", [DIN, 4 * 2 * DST], F32)
        slots1 = sb("slots1", [DIN, 4 * 2], F32)
        Stot_s = sb("Stot_s", [DIN, 1], F32)

        pA = es.enter_context(nc.psum_tensor("pA", [P, 2048], F32))
        pB = es.enter_context(nc.psum_tensor("pB", [P, 1024], F32))
        pC = es.enter_context(nc.psum_tensor("pC", [P, 512], F32))
        pD = es.enter_context(nc.psum_tensor("pD", [P, 512], F32))
        pB16 = pB[:].bitcast(BF16)
        pC16 = pC[:].bitcast(BF16)
        pD16 = pD[:].bitcast(BF16)
        tslots = [pB16[:, 0:P], pB16[:, 1024:1024 + P],
                  pC16[:, 0:P], pD16[:, 2 * P:3 * P]]

        S = {}
        for name in ("ld", "gld", "wg", "wgf", "wu", "wuf", "wd", "wdf",
                     "pg", "pu", "sg", "h", "pd", "xw", "trp", "tre",
                     "lntok", "rt", "rcp", "stat",
                     "mA", "mB", "mC", "mD", "mE", "mF", "mG", "mH",
                     "bcl", "bcf", "ag", "st"):
            S[name] = Sem(es, nc, "s_" + name)

        sync_loads = [
            (xr[:].rearrange("p (a f) -> p a f", a=NT),
             dx.rearrange("(a p) f -> p a f", p=P)),
            (xpB_s[:], dxpB), (xpC_s[:], dxpC), (xpd_s[:], dxpd),
            (dtT_s[:], ddtT), (dtb_s[:], ddtb), (convw_s[:], dconvw),
            (convb_s[:], dconvb), (Alog_s[:], dAlog), (Dp_s[:], dDp),
            (selc_s[:], dselc), (selL_s[:], dselL), (selR_s[:], dselR),
        ]
        for dst, src in sync_loads:
            pl.op("sync", lambda e, d_=dst, s_=src: e.dma_start(out=d_, in_=s_),
                  inc=(S["ld"], 16))
        LD0 = S["ld"].n
        g_loads = [
            (ident_s[:], dident),
            (gate_s[:].rearrange("p (a f) -> p a f", a=KT),
             dgate.rearrange("(a p) f -> p a f", p=P)),
            (inp_s[:].rearrange("p (a f) -> p a f", a=KT),
             dinp.rearrange("(a p) f -> p a f", p=P)),
            (outT_s[:], doutT),
        ]
        for dst, src in g_loads:
            pl.op("gpsimd", lambda e, d_=dst, s_=src: e.dma_start(out=d_, in_=s_),
                  inc=(S["gld"], 16))
        GLD0 = S["gld"].n

        pl.wait("scalar", S["ld"], LD0)
        pl.op("scalar", lambda e: e.activation(negA[:], Alog_s[:], AF.Exp),
              inc=(S["stat"], 1))
        pl.wait("vector", S["stat"], S["stat"].n)
        pl.op("vector", lambda e: e.tensor_scalar_mul(negA[:], negA[:], -1.0),
              inc=(S["stat"], 1))

        C = dict(ug=0, ud=0, gsl=0, usl=0, wsl=0, ne=0, tr=0)
        state = dict(stt_val=[], tile_last_stt={})

        def emit_transposes(src_tile_fn, nt_tiles, prep_fn, xw_entry, rcp_entry):
            base = S["tre"].n
            first = True
            for tt in range(nt_tiles):
                psem, pval = prep_fn(tt)
                pl.wait("tensor", psem, pval)
                if first:
                    pl.wait("tensor", S["xw"], xw_entry)
                    pl.wait("tensor", S["rcp"], rcp_entry)
                    first = False
                for kt in range(KT):
                    k = C["tr"]
                    sl = k % 4
                    if k >= 4:
                        pl.wait("tensor", S["tre"], k - 3)
                    pl.op("tensor", lambda e, s_=src_tile_fn(tt, kt), d_=tslots[sl]:
                          e.transpose(d_, s_, ident_s[:]), inc=(S["trp"], 1))
                    eng = "scalar" if (kt % 2 == 0) else "vector"
                    pl.wait(eng, S["trp"], S["trp"].n)
                    dst = lnT[:, kt * T + tt * P: kt * T + (tt + 1) * P]
                    if eng == "scalar":
                        pl.op(eng, lambda e, d_=dst, s_=tslots[sl]: e.copy(d_, s_),
                              inc=(S["tre"], 1))
                    else:
                        pl.op(eng, lambda e, d_=dst, s_=tslots[sl]:
                              e.tensor_copy(d_, s_), inc=(S["tre"], 1))
                    C["tr"] += 1
            return base

        def ffn_call(ph, xw_entry, rcp_entry):
            xw_base = S["xw"].n

            pl.wait("vector", S["ld"], 16)
            for tt in range(NT):
                pl.op("vector", lambda e, t=tt: e.reduce_sum(
                    fsum[:, t:t + 1], xr[:, t * D:(t + 1) * D], axis=AXX),
                    inc=(S["stat"], 1))
            pl.wait("scalar", S["ld"], 16)
            pl.wait("scalar", S["xw"], xw_entry)
            dump = lnT[:, 0:D]
            for tt in range(NT):
                pl.op("scalar", lambda e, t=tt: e.activation(
                    dump, xr[:, t * D:(t + 1) * D], AF.Square,
                    accum_out=fsq[:, t:t + 1]),
                    inc=(S["stat"], 1))
            pl.wait("vector", S["stat"], S["stat"].n)
            pl.op("vector", lambda e: e.tensor_scalar_mul(fmu[:], fsum[:], 1.0 / D))
            pl.op("vector", lambda e: e.tensor_scalar_mul(fvar[:], fsq[:], 1.0 / D))
            pl.raw("vector", lambda e: e.drain())
            pl.op("vector", lambda e: e.tensor_mul(ftmp[:], fmu[:], fmu[:]))
            pl.raw("vector", lambda e: e.drain())
            pl.op("vector", lambda e: e.tensor_sub(fvar[:], fvar[:], ftmp[:]))
            pl.raw("vector", lambda e: e.drain())
            pl.op("vector", lambda e: e.tensor_scalar_add(ftmp[:], fvar[:], EPS),
                  inc=(S["stat"], 1))
            pl.wait("scalar", S["stat"], S["stat"].n)
            pl.op("scalar", lambda e: e.activation(ftmp[:], ftmp[:], AF.Sqrt),
                  inc=(S["stat"], 1))
            pl.wait("vector", S["stat"], S["stat"].n)
            pl.op("vector", lambda e: e.reciprocal(frstd[:], ftmp[:]))
            pl.raw("vector", lambda e: e.drain())
            pl.op("vector", lambda e: e.tensor_mul(fnmu[:], fmu[:], frstd[:]))
            pl.raw("vector", lambda e: e.drain())
            pl.op("vector", lambda e: e.tensor_scalar_mul(fnmu[:], fnmu[:], -1.0),
                  inc=(S["stat"], 1))
            stats_done = S["stat"].n

            ln1tok = state["ln1tok"]

            def prep(tt):
                if tt == 0:
                    pl.wait("scalar", S["stat"], stats_done)
                if tt >= 1:
                    pl.wait("scalar", S["trp"], C["tr"])
                pl.op("scalar", lambda e, t=tt: e.activation(
                    ln1tok[:, 0:D],
                    xr[:, t * D:(t + 1) * D], AF.Identity,
                    bias=fnmu[:, t:t + 1], scale=frstd[:, t:t + 1]),
                    inc=(S["lntok"], 1))
                return S["lntok"], S["lntok"].n

            def srcf(tt, kt):
                return ln1tok[:, kt * P:(kt + 1) * P]

            tre_base = emit_transposes(srcf, NT, prep, xw_entry, rcp_entry)

            pl.wait("tensor", S["gld"], GLD0)
            pl.wait("tensor", S["tre"], tre_base + NT * KT)
            for tt in range(NT):
                for kt in range(KT):
                    pl.op("tensor", lambda e, t=tt, k=kt: e.matmul(
                        pD[:, t * 8:(t + 1) * 8],
                        lnT[:, k * T + t * P: k * T + (t + 1) * P],
                        gate_s[:, k * 8:(k + 1) * 8],
                        start=(k == 0), stop=(k == KT - 1)),
                        inc=(S["rt"], 1) if kt == KT - 1 else None)
            pl.wait("vector", S["rt"], S["rt"].n)
            pl.op("vector", lambda e: e.tensor_copy(lg_s[:], pD[:, 0:NT * 8]),
                  inc=(S["rcp"], 1))
            pl.raw("vector", lambda e: e.drain())
            lg3 = lg_s[:].rearrange("p (a f) -> p a f", a=NT)
            pl.op("vector", lambda e: e.reduce_max(m1_s[:], lg3, axis=AXX))
            pl.raw("vector", lambda e: e.drain())
            for tt in range(NT):
                pl.op("vector", lambda e, t=tt: e.tensor_scalar(
                    eq1_s[:, t * 8:(t + 1) * 8], lg_s[:, t * 8:(t + 1) * 8],
                    m1_s[:, t:t + 1], None, ALU.is_equal))
            pl.raw("vector", lambda e: e.drain())
            for tt in range(NT):
                pl.op("vector", lambda e, t=tt: e.scalar_tensor_tensor(
                    msk_s[:, t * 8:(t + 1) * 8], eq1_s[:, t * 8:(t + 1) * 8],
                    -1e30, lg_s[:, t * 8:(t + 1) * 8], ALU.mult, ALU.add))
            pl.raw("vector", lambda e: e.drain())
            msk3 = msk_s[:].rearrange("p (a f) -> p a f", a=NT)
            pl.op("vector", lambda e: e.reduce_max(m2_s[:], msk3, axis=AXX))
            pl.raw("vector", lambda e: e.drain())
            pl.op("vector", lambda e: e.tensor_sub(e2_s[:], m2_s[:], m1_s[:]),
                  inc=(S["stat"], 1))
            pl.wait("scalar", S["stat"], S["stat"].n)
            pl.op("scalar", lambda e: e.activation(e2_s[:], e2_s[:], AF.Exp),
                  inc=(S["stat"], 1))
            pl.wait("vector", S["stat"], S["stat"].n)
            pl.op("vector", lambda e: e.tensor_scalar_add(w1_s[:], e2_s[:], 1.0))
            pl.raw("vector", lambda e: e.drain())
            pl.op("vector", lambda e: e.reciprocal(w1_s[:], w1_s[:]))
            pl.raw("vector", lambda e: e.drain())
            pl.op("vector", lambda e: e.tensor_scalar_mul(w1_s[:], w1_s[:], 0.5))
            pl.raw("vector", lambda e: e.drain())
            pl.op("vector", lambda e: e.tensor_mul(w2_s[:], e2_s[:], w1_s[:]))
            pl.raw("vector", lambda e: e.drain())
            for tt in range(NT):
                pl.op("vector", lambda e, t=tt: e.tensor_scalar(
                    msk_s[:, t * 8:(t + 1) * 8], msk_s[:, t * 8:(t + 1) * 8],
                    m2_s[:, t:t + 1], None, ALU.is_equal))
            pl.raw("vector", lambda e: e.drain())
            for tt in range(NT):
                pl.op("vector", lambda e, t=tt: e.tensor_scalar(
                    msk_s[:, t * 8:(t + 1) * 8], msk_s[:, t * 8:(t + 1) * 8],
                    w2_s[:, t:t + 1], None, ALU.mult))
            pl.raw("vector", lambda e: e.drain())
            for tt in range(NT):
                pl.op("vector", lambda e, t=tt: e.scalar_tensor_tensor(
                    comb[:, t * 8:(t + 1) * 8], eq1_s[:, t * 8:(t + 1) * 8],
                    w1_s[:, t:t + 1], msk_s[:, t * 8:(t + 1) * 8],
                    ALU.mult, ALU.add))
            pl.raw("vector", lambda e: e.drain())

            h = state["h"]
            wd = state["wd"]
            def emit_slice(e_i, i):
                for (dsrc, buf, cnt, sld, slf) in (
                        (dwg, wg2, "gsl", "wg", "pg"),
                        (dwu, wu2, "usl", "wu", "pu")):
                    sl = C[cnt]
                    if sl >= 3:
                        pl.wait("gpsimd", S[slf], 2 * (sl - 3) + 2)
                    src = dsrc[e_i][:, i * P:(i + 1) * P].rearrange(
                        "(a p) i -> p a i", p=P)
                    dst = buf[:, (sl % 3) * KT * P:((sl % 3) + 1) * KT * P] \
                        .rearrange("p (a i) -> p a i", a=KT)
                    pl.op("gpsimd", lambda e, d_=dst, s_=src:
                          e.dma_start(out=d_, in_=s_), inc=(S[sld], 16))
                    C[cnt] += 1

            for e_i in range(n_experts):
                shared = (e_i == NE - 1)
                emit_slice(e_i, 0)
                emit_slice(e_i, 1)
                if C["ne"] >= 1:
                    pl.wait("gpsimd", S["pd"], 32 * C["ne"])
                if ph == 1 and e_i == 0:
                    pl.wait("gpsimd", S["xw"], xw_entry)
                for i in range(IT):
                    src = dwd[e_i][i * P:(i + 1) * P, :]
                    dst = wd[:, i * D:(i + 1) * D]
                    pl.op("gpsimd", lambda e, d_=dst, s_=src:
                          e.dma_start(out=d_, in_=s_), inc=(S["wd"], 16))
                    C["wsl"] += 1
                for i in range(2, IT):
                    emit_slice(e_i, i)

                for i in range(IT):
                    for tc in range(2):
                        ug = C["ug"]
                        gsl_idx = C["ne"] * IT + i
                        pl.wait("tensor", S["wg"], 16 * (gsl_idx + 1))
                        if ug >= 2:
                            pl.wait("tensor", S["sg"], ug - 1)
                        if i == 0 and e_i == 0:
                            pl.wait("tensor", S["tre"],
                                    tre_base + (4 if tc == 0 else 8) * KT)
                        gslot = pA[:, (ug % 2) * 512:((ug % 2) + 1) * 512]
                        for kt in range(KT):
                            pl.op("tensor", lambda e, k=kt, t=tc, sl=gslot,
                                  b=(gsl_idx % 3): e.matmul(
                                      sl,
                                      wg2[:, b * KT * P + k * P:
                                          b * KT * P + (k + 1) * P],
                                      lnT[:, k * T + t * 512:
                                          k * T + (t + 1) * 512],
                                      start=(k == 0), stop=(k == KT - 1)),
                                  inc=(S["pg"], 1) if kt == KT - 1 else None)
                        pl.wait("tensor", S["wu"], 16 * (gsl_idx + 1))
                        if ug >= 2:
                            pl.wait("tensor", S["h"], ug - 1)
                        uslot = pA[:, 1024 + (ug % 2) * 512:
                                   1024 + ((ug % 2) + 1) * 512]
                        for kt in range(KT):
                            pl.op("tensor", lambda e, k=kt, t=tc, sl=uslot,
                                  b=(gsl_idx % 3): e.matmul(
                                      sl,
                                      wu2[:, b * KT * P + k * P:
                                          b * KT * P + (k + 1) * P],
                                      lnT[:, k * T + t * 512:
                                          k * T + (t + 1) * 512],
                                      start=(k == 0), stop=(k == KT - 1)),
                                  inc=(S["pu"], 1) if kt == KT - 1 else None)
                        pl.wait("scalar", S["pg"], ug + 1)
                        if ug >= 2:
                            pl.wait("scalar", S["h"], ug - 1)
                        sgslot = sg2[:, (ug % 2) * 512:((ug % 2) + 1) * 512]
                        pl.op("scalar", lambda e, sl=sgslot, g=gslot:
                              e.activation(sl, g, AF.Silu), inc=(S["sg"], 1))
                        pl.wait("vector", S["pu"], ug + 1)
                        pl.wait("vector", S["sg"], ug + 1)
                        hdst = h[:, i * T + tc * 512: i * T + (tc + 1) * 512]
                        pl.op("vector", lambda e, d_=hdst, sl=sgslot, u=uslot:
                              e.tensor_mul(d_, sl, u), inc=(S["h"], 1))
                        C["ug"] += 1

                pl.wait("tensor", S["h"], C["ug"])
                pl.wait("tensor", S["wd"], 16 * C["wsl"])
                if e_i == 0:
                    pl.wait("tensor", S["tre"], tre_base + NT * KT)
                for t128 in range(NT):
                    for hh in range(4):
                        ud = C["ud"]
                        if ud >= 2:
                            pl.wait("tensor", S["xw"], state["stt_val"][ud - 2])
                        dslot = pB[:, (ud % 2) * 512:((ud % 2) + 1) * 512]
                        for i in range(IT):
                            pl.op("tensor", lambda e, t=t128, q=hh, ii=i, sl=dslot:
                                  e.matmul(
                                      sl,
                                      h[:, ii * T + t * P: ii * T + (t + 1) * P],
                                      wd[:, ii * D + q * 512:
                                         ii * D + (q + 1) * 512],
                                      start=(ii == 0), stop=(ii == IT - 1)),
                                  inc=(S["pd"], 1) if i == IT - 1 else None)
                        pl.wait("vector", S["pd"], S["pd"].n)
                        xv = xr[:, t128 * D + hh * 512: t128 * D + (hh + 1) * 512]
                        if shared:
                            v = pl.op("vector", lambda e, x=xv, sl=dslot:
                                      e.scalar_tensor_tensor(
                                          x, sl, 0.5, x, ALU.mult, ALU.add),
                                      inc=(S["xw"], 1))
                        else:
                            cc = comb[:, t128 * 8 + e_i: t128 * 8 + e_i + 1]
                            v = pl.op("vector", lambda e, x=xv, sl=dslot, c=cc:
                                      e.scalar_tensor_tensor(
                                          x, sl, c, x, ALU.mult, ALU.add),
                                      inc=(S["xw"], 1))
                        state["stt_val"].append(v)
                        if e_i == n_experts - 1 and hh == 3:
                            state["tile_last_stt"][(ph, t128)] = v
                        C["ud"] += 1
                C["ne"] += 1

        g_h1 = nc.sbuf_tensor("h1", [P, IT * T], BF16)
        g_tok1 = nc.sbuf_tensor("tok1", [P, D], BF16)
        g_wd1 = nc.sbuf_tensor("wd1", [P, IT * D], BF16)
        state["h"] = g_h1.__enter__()
        state["ln1tok"] = g_tok1.__enter__()
        state["wd"] = g_wd1.__enter__()
        ffn_call(0, xw_entry=0, rcp_entry=0)
        F1_XW = S["xw"].n
        g_wd1.__exit__(None, None, None)
        g_tok1.__exit__(None, None, None)
        g_h1.__exit__(None, None, None)
        mamba_xw = F1_XW

        if do_mamba:
            m_guards = []

            def msb(name, shape, dt):
                g = nc.sbuf_tensor(name, list(shape), dt)
                m_guards.append(g)
                return g.__enter__()

            zz = msb("zz", [P, D], BF16)
            wbc = msb("wbc", [P, D], BF16)
            bbc = msb("bbc", [P, D], BF16)
            xin = msb("xin", [DIN, T + 2], F32)
            u_s = msb("u_s", [DIN, T], F32)
            delta = msb("delta", [DIN, T], F32)
            dA0 = msb("dA0", [DIN, T], F32)
            dA1 = msb("dA1", [DIN, T], F32)
            dBu = msb("dBu", [DIN, T], F32)
            xsc = msb("xsc", [DIN, T], F32)
            y_s = msb("y_s", [DIN, T], F32)
            Bp_s = msb("Bp_s", [DST, T], F32)
            Cp_s = msb("Cp_s", [DST, T], F32)
            y16 = msb("y16", [DIN, T], BF16)
            bcB = msb("bcB", [DIN, 2 * T], F32)
            bcC = msb("bcC", [DIN, 2 * T], F32)
            dAbufs = [dA0, dA1]

            pl.wait("gpsimd", S["xw"], F1_XW)
            pl.op("gpsimd", lambda e: e.dma_start(out=wbc[:], in_=dlnw),
                  inc=(S["gld"], 16))
            pl.op("gpsimd", lambda e: e.dma_start(out=bbc[:], in_=dlnb),
                  inc=(S["gld"], 16))
            GLDW = S["gld"].n

            for tt in range(NT):
                pl.op("vector", lambda e, t=tt: e.reduce_sum(
                    fsum[:, t:t + 1], xr[:, t * D:(t + 1) * D], axis=AXX),
                    inc=(S["mA"], 1))
            pl.wait("scalar", S["xw"], F1_XW)
            for tt in range(NT):
                pl.op("scalar", lambda e, t=tt: e.activation(
                    zz[:, 0:D], xr[:, t * D:(t + 1) * D], AF.Square,
                    accum_out=fsq[:, t:t + 1]),
                    inc=(S["mA"], 1))
            pl.wait("vector", S["mA"], S["mA"].n)
            pl.op("vector", lambda e: e.tensor_scalar_mul(mmu[:], fsum[:], 1.0 / D))
            pl.op("vector", lambda e: e.tensor_scalar_mul(mvar[:], fsq[:], 1.0 / D))
            pl.raw("vector", lambda e: e.drain())
            pl.op("vector", lambda e: e.tensor_mul(mtmp[:], mmu[:], mmu[:]))
            pl.raw("vector", lambda e: e.drain())
            pl.op("vector", lambda e: e.tensor_sub(mvar[:], mvar[:], mtmp[:]))
            pl.raw("vector", lambda e: e.drain())
            pl.op("vector", lambda e: e.tensor_scalar_add(mtmp[:], mvar[:], EPS),
                  inc=(S["mB"], 1))
            pl.wait("scalar", S["mB"], S["mB"].n)
            pl.op("scalar", lambda e: e.activation(mtmp[:], mtmp[:], AF.Sqrt),
                  inc=(S["mB"], 1))
            pl.wait("vector", S["mB"], S["mB"].n)
            pl.op("vector", lambda e: e.reciprocal(mrstd[:], mtmp[:]))
            pl.raw("vector", lambda e: e.drain())
            pl.op("vector", lambda e: e.tensor_mul(mtmp[:], mvar[:], mrstd[:]))
            pl.raw("vector", lambda e: e.drain())
            pl.op("vector", lambda e: e.tensor_mul(mtmp[:], mtmp[:], mrstd[:]))
            pl.raw("vector", lambda e: e.drain())
            pl.op("vector", lambda e: e.tensor_scalar_add(mtmp[:], mtmp[:], EPS),
                  inc=(S["mB"], 1))
            pl.wait("scalar", S["mB"], S["mB"].n)
            pl.op("scalar", lambda e: e.activation(mtmp[:], mtmp[:], AF.Sqrt),
                  inc=(S["mB"], 1))
            pl.wait("vector", S["mB"], S["mB"].n)
            pl.op("vector", lambda e: e.reciprocal(mtmp[:], mtmp[:]))
            pl.raw("vector", lambda e: e.drain())
            pl.op("vector", lambda e: e.tensor_mul(mseff[:], mrstd[:], mtmp[:]))
            pl.raw("vector", lambda e: e.drain())
            pl.op("vector", lambda e: e.tensor_mul(mnb[:], mmu[:], mseff[:]))
            pl.op("vector", lambda e: e.tensor_mul(mnmr[:], mmu[:], mrstd[:]))
            pl.raw("vector", lambda e: e.drain())
            pl.op("vector", lambda e: e.tensor_scalar_mul(mnb[:], mnb[:], -1.0))
            pl.op("vector", lambda e: e.tensor_scalar_mul(mnmr[:], mnmr[:], -1.0),
                  inc=(S["mB"], 1))
            mstats = S["mB"].n

            def mprep(tt):
                if tt == 0:
                    pl.wait("scalar", S["mB"], mstats)
                    pl.wait("scalar", S["gld"], GLDW)
                if tt >= 1:
                    pl.wait("scalar", S["trp"], C["tr"])
                pl.op("scalar", lambda e, t=tt: e.activation(
                    zz[:, 0:D],
                    xr[:, t * D:(t + 1) * D], AF.Identity,
                    bias=mnb[:, t:t + 1], scale=mseff[:, t:t + 1]),
                    inc=(S["mC"], 1))
                pl.wait("vector", S["mC"], S["mC"].n)
                zv = zz[:, 0:D]
                pl.op("vector", lambda e, z=zv: e.tensor_mul(z, z, wbc[:]))
                pl.op("vector", lambda e, z=zv: e.tensor_add(z, z, bbc[:]),
                      inc=(S["mC"], 1))
                return S["mC"], S["mC"].n

            def msrc(tt, kt):
                return zz[:, kt * P:(kt + 1) * P]

            tre_base2 = emit_transposes(msrc, NT, mprep, F1_XW, 1)

            pl.wait("tensor", S["tre"], tre_base2 + NT * KT)
            pl.wait("tensor", S["gld"], GLD0)
            for tc in range(2):
                for kt in range(KT):
                    pl.op("tensor", lambda e, k=kt, t=tc: e.matmul(
                        pA[0:DIN, t * 512:(t + 1) * 512],
                        inp_s[:, k * DIN:(k + 1) * DIN],
                        lnT[:, k * T + t * 512: k * T + (t + 1) * 512],
                        start=(k == 0), stop=(k == KT - 1)),
                        inc=(S["mD"], 1) if (kt == KT - 1) else None)
            pl.wait("scalar", S["mD"], S["mD"].n)
            pl.op("scalar", lambda e: e.copy(xin[:, 1:T + 1], pA[0:DIN, 0:T]),
                  inc=(S["mD"], 1))
            inproj_done = S["mD"].n

            pl.wait("sync", S["mD"], inproj_done)
            pl.op("sync", lambda e: e.dma_start(
                out=dag1i[:, 0:1], in_=xin[:, 1:2],
                allow_slow_non_contiguous=True), inc=(S["ag"], 16))
            pl.op("sync", lambda e: e.dma_start(
                out=dag1i[:, 1:2], in_=xin[:, T:T + 1],
                allow_slow_non_contiguous=True), inc=(S["ag"], 16))
            pl.wait("gpsimd", S["ag"], S["ag"].n)
            pl.op("gpsimd", lambda e: e.collective_compute(
                "AllGather", ALU.bypass, replica_groups=groups,
                ins=[dag1i], outs=[dag1o]), inc=(S["ag"], 1))
            pl.wait("gpsimd", S["ag"], S["ag"].n)
            pl.op("gpsimd", lambda e: e.dma_start(
                out=slots1[:].rearrange("p (r f) -> p r f", r=4),
                in_=dag1o.rearrange("(r p) f -> p r f", r=4)),
                inc=(S["ag"], 16))
            pl.wait("vector", S["ag"], S["ag"].n)
            pl.op("vector", lambda e: e.memset(xin[:, 0:1], 0.0))
            pl.op("vector", lambda e: e.memset(xin[:, T + 1:T + 2], 0.0))
            pl.raw("vector", lambda e: e.drain())
            sl1 = slots1[:].rearrange("p (r f) -> p r f", r=4)
            for r in range(4):
                pl.op("vector", lambda e, rr=r: e.scalar_tensor_tensor(
                    xin[:, 0:1], sl1[:, rr, 1:2], selL_s[:, rr:rr + 1],
                    xin[:, 0:1], ALU.mult, ALU.add))
                pl.raw("vector", lambda e: e.drain())
                pl.op("vector", lambda e, rr=r: e.scalar_tensor_tensor(
                    xin[:, T + 1:T + 2], sl1[:, rr, 0:1], selR_s[:, rr:rr + 1],
                    xin[:, T + 1:T + 2], ALU.mult, ALU.add))
                pl.raw("vector", lambda e: e.drain())

            pl.op("vector", lambda e: e.tensor_scalar(
                delta[:], xin[:, 0:T], convw_s[:, 0:1], None, ALU.mult))
            pl.op("vector", lambda e: e.scalar_tensor_tensor(
                delta[:], xin[:, 1:T + 1], convw_s[:, 1:2], delta[:],
                ALU.mult, ALU.add))
            pl.op("vector", lambda e: e.scalar_tensor_tensor(
                delta[:], xin[:, 2:T + 2], convw_s[:, 2:3], delta[:],
                ALU.mult, ALU.add), inc=(S["mD"], 1))
            pl.wait("scalar", S["mD"], S["mD"].n)
            pl.op("scalar", lambda e: e.activation(
                u_s[:], delta[:], AF.Silu, bias=convb_s[:]), inc=(S["mD"], 1))
            u_done = S["mD"].n

            pl.wait("tensor", S["mD"], u_done)
            for tc in range(2):
                pl.op("tensor", lambda e, t=tc: e.matmul(
                    pA[0:DST, 1024 + t * 512:1024 + (t + 1) * 512],
                    xpB_s[:], u_s[:, t * 512:(t + 1) * 512],
                    start=True, stop=True),
                    inc=(S["mE"], 1) if tc == 1 else None)
                pl.op("tensor", lambda e, t=tc: e.matmul(
                    pB[0:DST, t * 512:(t + 1) * 512],
                    xpC_s[:], u_s[:, t * 512:(t + 1) * 512],
                    start=True, stop=True),
                    inc=(S["mE"], 1) if tc == 1 else None)
            pl.wait("scalar", S["mE"], S["mE"].n)
            pl.op("scalar", lambda e: e.copy(Bp_s[:], pA[0:DST, 1024:2048]),
                  inc=(S["mE"], 1))
            pl.op("scalar", lambda e: e.copy(Cp_s[:], pB[0:DST, 0:T]),
                  inc=(S["mE"], 1))
            bc_evac = S["mE"].n
            pl.wait("tensor", S["mE"], bc_evac)
            for tc in range(2):
                pl.op("tensor", lambda e, t=tc: e.matmul(
                    pA[0:DTR, 1024 + t * 512:1024 + (t + 1) * 512],
                    xpd_s[:], u_s[:, t * 512:(t + 1) * 512],
                    start=True, stop=True),
                    inc=(S["mE"], 1) if tc == 1 else None)
            pl.wait("scalar", S["mE"], S["mE"].n)
            pl.op("scalar", lambda e: e.copy(xsc[0:DTR, :], pA[0:DTR, 1024:2048]),
                  inc=(S["mE"], 1))
            dr_evac = S["mE"].n
            pl.wait("tensor", S["mE"], dr_evac)
            for tc in range(2):
                pl.op("tensor", lambda e, t=tc: e.matmul(
                    pA[0:DIN, t * 512:(t + 1) * 512],
                    dtT_s[:], xsc[0:DTR, t * 512:(t + 1) * 512],
                    start=True, stop=True),
                    inc=(S["mE"], 1) if tc == 1 else None)
            pl.wait("scalar", S["mE"], S["mE"].n)
            pl.op("scalar", lambda e: e.activation(
                dA0[:], pA[0:DIN, 0:T], AF.Abs, bias=dtb_s[:]))
            pl.op("scalar", lambda e: e.activation(
                dA1[:], dA0[:], AF.Exp, scale=-1.0))
            pl.op("scalar", lambda e: e.activation(
                dA1[:], dA1[:], AF.Ln, bias=1.0))
            pl.op("scalar", lambda e: e.activation(
                dA0[:], pA[0:DIN, 0:T], AF.Relu, bias=dtb_s[:]),
                inc=(S["mE"], 1))
            pl.wait("vector", S["mE"], S["mE"].n)
            pl.op("vector", lambda e: e.tensor_add(delta[:], dA0[:], dA1[:]))
            pl.op("vector", lambda e: e.tensor_scalar_mul(delta[:], delta[:],
                                                          0.001))
            pl.raw("vector", lambda e: e.drain())
            pl.op("vector", lambda e: e.reduce_sum(Stot_s[:], delta[:], axis=AXX))
            pl.raw("vector", lambda e: e.drain())
            pl.op("vector", lambda e: e.tensor_scalar(
                Atot_s[:], negA[:], Stot_s[:], None, ALU.mult), inc=(S["mF"], 1))
            pl.wait("scalar", S["mF"], S["mF"].n)
            pl.op("scalar", lambda e: e.activation(Atot_s[:], Atot_s[:], AF.Exp),
                  inc=(S["mF"], 1))
            prep_done = S["mF"].n

            pl.wait("sync", S["mE"], bc_evac)
            pl.op("sync", lambda e: e.dma_start(out=dbpcp[0:DST, :], in_=Bp_s[:]),
                  inc=(S["bcl"], 16))
            pl.op("sync", lambda e: e.dma_start(out=dbpcp[DST:2 * DST, :],
                                                in_=Cp_s[:]),
                  inc=(S["bcl"], 16))

            nbc = 0
            for n in range(DST):
                if nbc >= 2:
                    pl.wait("sync", S["bcf"], nbc - 1)
                bslot = bcB[:, (n % 2) * T:((n % 2) + 1) * T]
                pl.op("sync", lambda e, d_=bslot, r=n: e.dma_start(
                    out=d_, in_=dbpcp[r:r + 1, :].broadcast_to([DIN, T])),
                    inc=(S["bcl"], 16))
                nbc += 1
                if n >= 2:
                    pl.wait("scalar", S["mG"], 3 * (n - 2) + 2)
                pl.op("scalar", lambda e, r=n, b=dAbufs[n % 2]: e.activation(
                    b[:], delta[:], AF.Exp, scale=negA[:, r:r + 1]),
                    inc=(S["mF"], 1))
                pl.wait("vector", S["mF"], prep_done + n + 1)
                pl.op("vector", lambda e: e.tensor_mul(dBu[:], delta[:], u_s[:]))
                pl.wait("vector", S["bcl"], 32 + 16 * nbc)
                pl.op("vector", lambda e, b=bslot: e.tensor_mul(dBu[:], dBu[:], b),
                      inc=(S["bcf"], 1), inc2=(S["mG"], 1))
                pl.op("vector", lambda e, b=dAbufs[n % 2]: e.tensor_tensor_scan(
                    xsc[:], b[:], dBu[:], 0.0, ALU.mult, ALU.add),
                    inc=(S["mG"], 1))
                pl.raw("vector", lambda e: e.drain())
                pl.op("vector", lambda e, r=n: e.tensor_copy(
                    F_s[:, r:r + 1], xsc[:, T - 1:T]), inc=(S["mG"], 1))
                pl.raw("vector", lambda e: e.drain())
            pass1_done = S["mG"].n

            pl.wait("sync", S["mG"], pass1_done)
            pl.op("sync", lambda e: e.dma_start(out=dag2i[:, 0:DST], in_=F_s[:]),
                  inc=(S["ag"], 16))
            pl.op("sync", lambda e: e.dma_start(out=dag2i[:, DST:2 * DST],
                                                in_=Atot_s[:]),
                  inc=(S["ag"], 16))
            pl.wait("gpsimd", S["ag"], S["ag"].n)
            pl.op("gpsimd", lambda e: e.collective_compute(
                "AllGather", ALU.bypass, replica_groups=groups,
                ins=[dag2i], outs=[dag2o]), inc=(S["ag"], 1))
            pl.wait("gpsimd", S["ag"], S["ag"].n)
            pl.op("gpsimd", lambda e: e.dma_start(
                out=slots2[:].rearrange("p (r f) -> p r f", r=4),
                in_=dag2o.rearrange("(r p) f -> p r f", r=4)),
                inc=(S["ag"], 16))
            pl.wait("vector", S["ag"], S["ag"].n)
            sl2 = slots2[:].rearrange("p (r f) -> p r f", r=4)
            pl.op("vector", lambda e: e.tensor_copy(cs1[:], sl2[:, 0, 0:DST]))
            pl.raw("vector", lambda e: e.drain())
            pl.op("vector", lambda e: e.tensor_mul(ctmp[:],
                                                   sl2[:, 1, DST:2 * DST], cs1[:]))
            pl.raw("vector", lambda e: e.drain())
            pl.op("vector", lambda e: e.tensor_add(cs2[:], ctmp[:],
                                                   sl2[:, 1, 0:DST]))
            pl.raw("vector", lambda e: e.drain())
            pl.op("vector", lambda e: e.tensor_mul(ctmp[:],
                                                   sl2[:, 2, DST:2 * DST], cs2[:]))
            pl.raw("vector", lambda e: e.drain())
            pl.op("vector", lambda e: e.tensor_add(cs3[:], ctmp[:],
                                                   sl2[:, 2, 0:DST]))
            pl.raw("vector", lambda e: e.drain())
            pl.op("vector", lambda e: e.memset(carry[:], 0.0))
            pl.raw("vector", lambda e: e.drain())
            for j, csj in enumerate((cs1, cs2, cs3)):
                pl.op("vector", lambda e, c=csj, jj=j: e.scalar_tensor_tensor(
                    carry[:], c[:], selc_s[:, jj:jj + 1], carry[:],
                    ALU.mult, ALU.add))
                pl.raw("vector", lambda e: e.drain())
            pl.op("vector", lambda e: e.memset(y_s[:], 0.0), inc=(S["mG"], 1))

            for n in range(DST):
                if nbc >= 2:
                    pl.wait("sync", S["bcf"], nbc - 1)
                bslot = bcB[:, (n % 2) * T:((n % 2) + 1) * T]
                pl.op("sync", lambda e, d_=bslot, r=n: e.dma_start(
                    out=d_, in_=dbpcp[r:r + 1, :].broadcast_to([DIN, T])),
                    inc=(S["bcl"], 16))
                nbc += 1
                if nbc >= 2:
                    pl.wait("sync", S["bcf"], nbc - 1)
                cslot = bcC[:, (n % 2) * T:((n % 2) + 1) * T]
                pl.op("sync", lambda e, d_=cslot, r=n: e.dma_start(
                    out=d_,
                    in_=dbpcp[DST + r:DST + r + 1, :].broadcast_to([DIN, T])),
                    inc=(S["bcl"], 16))
                nbc += 1
                if n >= 2:
                    pl.wait("scalar", S["mH"], 4 * (n - 2) + 2)
                pl.op("scalar", lambda e, r=n, b=dAbufs[n % 2]: e.activation(
                    b[:], delta[:], AF.Exp, scale=negA[:, r:r + 1]),
                    inc=(S["mF"], 1))
                pl.wait("vector", S["mF"], prep_done + DST + n + 1)
                pl.op("vector", lambda e: e.tensor_mul(dBu[:], delta[:], u_s[:]))
                pl.wait("vector", S["bcl"], 32 + 16 * nbc)
                pl.op("vector", lambda e, b=bslot: e.tensor_mul(dBu[:], dBu[:], b),
                      inc=(S["bcf"], 1), inc2=(S["mH"], 1))
                pl.op("vector", lambda e, b=dAbufs[n % 2], r=n:
                      e.tensor_tensor_scan(
                          xsc[:], b[:], dBu[:], carry[:, r:r + 1],
                          ALU.mult, ALU.add),
                      inc=(S["mH"], 1))
                pl.op("vector", lambda e, c=cslot: e.tensor_mul(dBu[:], xsc[:], c),
                      inc=(S["bcf"], 1), inc2=(S["mH"], 1))
                pl.op("vector", lambda e: e.tensor_add(y_s[:], y_s[:], dBu[:]),
                      inc=(S["mH"], 1))
            pl.op("vector", lambda e: e.scalar_tensor_tensor(
                y_s[:], u_s[:], Dp_s[:], y_s[:], ALU.mult, ALU.add))
            pl.op("vector", lambda e: e.tensor_copy(y16[:], y_s[:]),
                  inc=(S["mH"], 1))
            y_done = S["mH"].n

            pl.wait("tensor", S["mH"], y_done)
            for tt in range(NT):
                for q in range(4):
                    pl.op("tensor", lambda e, t=tt, qq=q: e.matmul(
                        pA[:, qq * 512:(qq + 1) * 512],
                        y16[:, t * P:(t + 1) * P],
                        outT_s[:, qq * 512:(qq + 1) * 512],
                        start=True, stop=True),
                        inc=(S["mH"], 1) if q == 3 else None)
                pl.wait("vector", S["mH"], S["mH"].n)
                xv = xr[:, tt * D:(tt + 1) * D]
                pl.op("vector", lambda e, x=xv, t=tt: e.scalar_tensor_tensor(
                    x, x, mrstd[:, t:t + 1], x, ALU.mult, ALU.add))
                pl.op("vector", lambda e, x=xv, t=tt: e.tensor_scalar(
                    x, x, mnmr[:, t:t + 1], None, ALU.add))
                vres = pl.op("vector", lambda e, x=xv: e.tensor_add(
                    x, x, pA[:, 0:D]), inc=(S["xw"], 1))
                pl.wait("tensor", S["xw"], vres)
            for g in reversed(m_guards):
                g.__exit__(None, None, None)
            mamba_xw = S["xw"].n

        if do_ffn2:
            g_h2 = nc.sbuf_tensor("h2", [P, IT * T], BF16)
            g_tok2 = nc.sbuf_tensor("tok2", [P, D], BF16)
            g_wd2 = nc.sbuf_tensor("wd2", [P, IT * D], BF16)
            state["h"] = g_h2.__enter__()
            state["ln1tok"] = g_tok2.__enter__()
            state["wd"] = g_wd2.__enter__()
            ffn_call(1, xw_entry=mamba_xw, rcp_entry=1)
            g_wd2.__exit__(None, None, None)
            g_tok2.__exit__(None, None, None)
            g_h2.__exit__(None, None, None)

        if do_ffn2:
            for tt in range(NT):
                pl.wait("sync", S["xw"], state["tile_last_stt"][(1, tt)])
                pl.op("sync", lambda e, t=tt: e.dma_start(
                    out=dy[t * P:(t + 1) * P, :], in_=xr[:, t * D:(t + 1) * D]),
                    inc=(S["st"], 16))
        else:
            pl.wait("sync", S["xw"], S["xw"].n)
            for tt in range(NT):
                pl.op("sync", lambda e, t=tt: e.dma_start(
                    out=dy[t * P:(t + 1) * P, :], in_=xr[:, t * D:(t + 1) * D]),
                    inc=(S["st"], 16))
        pl.wait("sync", S["st"], S["st"].n)

        with nc.Block() as block:
            block.tensor(lambda e: [f(e) for f in pl.q["tensor"]])
            block.vector(lambda e: [f(e) for f in pl.q["vector"]])
            block.scalar(lambda e: [f(e) for f in pl.q["scalar"]])
            block.sync(lambda e: [f(e) for f in pl.q["sync"]])
            block.gpsimd(lambda e: [f(e) for f in pl.q["gpsimd"]])

    return nc


def prep_inputs(inputs):
    f = np.ascontiguousarray
    w = {}
    eg = inputs["L0_ffn_expert_gate_w"]
    eu = inputs["L0_ffn_expert_up_w"]
    ed = inputs["L0_ffn_expert_down_w"]
    sg = inputs["L0_ffn_shared_gate_w"]
    su = inputs["L0_ffn_shared_up_w"]
    sd = inputs["L0_ffn_shared_down_w"]
    w["wgT"] = f(np.concatenate([eg.transpose(0, 2, 1), sg.T[None]], axis=0))
    w["wuT"] = f(np.concatenate([eu.transpose(0, 2, 1), su.T[None]], axis=0))
    w["wdT"] = f(np.concatenate([ed.transpose(0, 2, 1), sd.T[None]], axis=0))
    w["gateT"] = f(inputs["L0_ffn_gate_weight"].T)
    w["inpT"] = f(inputs["L0_attn_in_proj_W"].T)
    xp = inputs["L0_attn_x_proj_W"]
    w["xpd"] = f(xp[0:DTR].T)
    w["xpB"] = f(xp[DTR:DTR + DST].T)
    w["xpC"] = f(xp[DTR + DST:DTR + 2 * DST].T)
    w["dtT"] = f(inputs["L0_attn_dt_proj_W"].T)
    w["dtb"] = f(inputs["L0_attn_dt_proj_b"].reshape(DIN, 1))
    w["convw"] = f(inputs["L0_attn_conv_w"].reshape(DIN, 3))
    w["convb"] = f(inputs["L0_attn_conv_b"].reshape(DIN, 1))
    w["Alog"] = f(inputs["L0_attn_A_log"])
    w["Dp"] = f(inputs["L0_attn_D_param"].reshape(DIN, 1))
    w["outT"] = f(inputs["L0_attn_out_proj_W"].T)
    w["lnw_bc"] = f(np.broadcast_to(np.asarray(inputs["L0_attn_ln_w"])[None, :],
                                    (P, D)))
    w["lnb_bc"] = f(np.broadcast_to(np.asarray(inputs["L0_attn_ln_b"])[None, :],
                                    (P, D)))
    w["ident"] = np.eye(P, dtype=np.float32)
    for k in w:
        w[k] = np.asarray(w[k], dtype=np.float32)
    return w


def make_in_maps(x, w):
    in_maps = []
    for c in range(NCORES):
        b, ch = c // 4, c % 4
        m = dict(w)
        m["x"] = np.ascontiguousarray(x[b, ch * T:(ch + 1) * T, :])
        selc = np.zeros((DIN, 3), np.float32)
        if ch >= 1:
            selc[:, ch - 1] = 1.0
        selL = np.zeros((DIN, 4), np.float32)
        selR = np.zeros((DIN, 4), np.float32)
        if ch > 0:
            selL[:, ch - 1] = 1.0
        if ch < 3:
            selR[:, ch + 1] = 1.0
        m["selc"] = selc
        m["selL"] = selL
        m["selR"] = selR
        in_maps.append(m)
    return in_maps


def kernel(**inputs):
    x = np.asarray(inputs["L0_attn_feat_patch"], dtype=np.float32)
    w = prep_inputs(inputs)
    nc = build_program()
    in_maps = make_in_maps(x, w)
    res = run_bass_kernel_spmd(nc, in_maps, core_ids=list(range(NCORES)))
    out = np.empty((2, 4096, D), dtype=np.float32)
    for c in range(NCORES):
        b, ch = c // 4, c % 4
        out[b, ch * T:(ch + 1) * T, :] = np.asarray(res.results[c]["y"],
                                                    dtype=np.float32)
    return out


if __name__ == "__main__":
    import reference as _r
    inp = _r.setup_inputs()
    got = kernel(**{k: np.asarray(v) for k, v in inp.items()})
    want = np.asarray(_r.reference(**inp))
    err = np.linalg.norm(got - want) / np.linalg.norm(want)
    print("rel_err", err)
